# version 06
# baseline (speedup 1.0000x reference)
import numpy as np
from contextlib import ExitStack

import concourse.bass as bass
import concourse.mybir as mybir
from concourse.bass_utils import run_bass_kernel_spmd

F32 = mybir.dt.float32
BF16 = mybir.dt.bfloat16
ALU = mybir.AluOpType
AF = mybir.ActivationFunctionType
AX = mybir.AxisListType

D = 2048
NH = 16
T = 1024
NT = 8
KT = 16
NE = 18
NPAIR = 11
SCALE_QK = float(1.0 / np.sqrt(128.0))
EXP_BIAS = -10.0

PERMS = [[0, 2, 4, 6, 9, 11, 13, 15], [1, 3, 5, 7, 8, 10, 12, 14]]
QC_KTS = [8, 16]

ENGINES = ("tensor", "vector", "scalar", "gpsimd", "sync")


class Chan:
    __slots__ = ("name", "count", "sem")

    def __init__(self, name):
        self.name = name
        self.count = 0
        self.sem = None


class Prog:
    def __init__(self):
        self.ops = {e: [] for e in ENGINES}
        self.hwm = {e: {} for e in ENGINES}
        self.chans = {}

    def chan(self, name):
        if name not in self.chans:
            self.chans[name] = Chan(name)
        return self.chans[name]

    def emit(self, eng, fn, waits=(), bump=None, amt=1):
        ws = []
        for wt in waits:
            if wt is None:
                continue
            ch, tok = wt
            if ch is None or tok is None or tok <= 0:
                continue
            if self.hwm[eng].get(ch.name, 0) >= tok:
                continue
            self.hwm[eng][ch.name] = tok
            ws.append((ch, tok))
        tup = None
        if bump is not None:
            bump.count += amt
            tup = (bump, bump.count)
        self.ops[eng].append((ws, fn, bump, amt))
        return tup


def build_program(dbg=(), n_experts=NE, d_oc=4, d_t=NT):
    nc = bass.Bass("TRN2", target_bir_lowering=False)
    P = Prog()

    xq = nc.dram_tensor("xq", [T, D], F32, kind="ExternalInput").ap()
    xk = nc.dram_tensor("xk", [2048, D], F32, kind="ExternalInput").ap()
    Mq = nc.dram_tensor("Mq", [2, 128, 8 * 512], F32, kind="ExternalInput").ap()
    Wq = nc.dram_tensor("Wq", [NH, D, 128], F32, kind="ExternalInput").ap()
    Wk = nc.dram_tensor("Wk", [NH, D, 128], F32, kind="ExternalInput").ap()
    Wv4 = nc.dram_tensor("Wv4", [4, D, 512], F32, kind="ExternalInput").ap()
    Wo = nc.dram_tensor("Wo", [D, D], F32, kind="ExternalInput").ap()
    Gw = nc.dram_tensor("Gw", [D, 16], F32, kind="ExternalInput").ap()
    Ue = nc.dram_tensor("Ue", [16, D, 2816], F32, kind="ExternalInput").ap()
    Dn = nc.dram_tensor("Dn", [16, 1408, D], F32, kind="ExternalInput").ap()
    Su = nc.dram_tensor("Su", [D, 5632], F32, kind="ExternalInput").ap()
    Sd = nc.dram_tensor("Sd", [2816, D], F32, kind="ExternalInput").ap()
    y = nc.dram_tensor("y", [T, D], F32, kind="ExternalOutput").ap()
    dbg_out = {}
    for name in dbg:
        shp = {"accD": [T, D], "x1": [T, D], "cw": [128, NT * NE],
               "oh": [128, NH * T], "qf": [128, NH * T], "kf": [128, 2048],
               "vt": [128, KT * 512], "acc0": [T, D], "sv": [128, 16],
               "wob_d": [128, 16 * 512], "pd": [128, 512],
               "oh2": [128, NH * T], "ev0": [128, 512]}[name]
        dbg_out[name] = nc.dram_tensor("dbg_" + name, shp, F32,
                                       kind="ExternalOutput").ap()

    def etab(e):
        if e < 16:
            return (Ue[e], 0, 1408, Dn[e], e)
        if e == 16:
            return (Su, 0, 2816, Sd[0:1408, :], 16)
        return (Su, 1408, 4224, Sd[1408:2816, :], 17)

    c_pe = P.chan("pe")
    c_ve = P.chan("ve")
    c_ae = P.chan("ae")
    c_gp = P.chan("gp")
    c_sd = P.chan("sd")
    c_so = P.chan("so")
    dch = {k: P.chan("d_" + k) for k in
           ("xb0", "xb1", "wqb0", "wqb1", "wkb0", "wkb1", "wvb", "mld",
            "wob0", "wob1", "gwb", "upg0", "upg1", "upu0", "upu1",
            "dnb0", "dnb1", "dnb2", "dnb3")}

    def bar():
        return [(c_pe, c_pe.count), (c_ve, c_ve.count), (c_ae, c_ae.count)]

    es = ExitStack()
    with es:
        ident = es.enter_context(nc.sbuf_tensor("ident", [128, 128], BF16))
        ones = es.enter_context(nc.sbuf_tensor("ones", [128, 128], BF16))
        gwb = es.enter_context(nc.sbuf_tensor("gwb", [128, 256], BF16))
        cst = es.enter_context(nc.sbuf_tensor("cst", [128, 8], F32))
        zz4 = es.enter_context(nc.sbuf_tensor("zz4", [128, 4], F32))

        P.emit("gpsimd", lambda g: g.memset(ident[:], 0.0), bump=c_gp)
        w_ident = P.emit(
            "gpsimd",
            lambda g: g.affine_select(
                out=ident[:], in_=ident[:], compare_op=ALU.not_equal,
                fill=1.0, base=0, pattern=[[-1, 128]], channel_multiplier=1),
            waits=[(c_gp, 1)], bump=c_gp)
        w_ones = P.emit("gpsimd", lambda g: g.memset(ones[:], 1.0), bump=c_gp)
        w_zz4 = P.emit("gpsimd", lambda g: g.memset(zz4[:], 0.0), bump=c_gp)
        for i, val in enumerate((EXP_BIAS, 1e-5, -0.5, SCALE_QK)):
            w_cst = P.emit("gpsimd",
                           (lambda i_=i, v_=val:
                            (lambda g: g.memset(cst[:, i_:i_ + 1], v_)))(),
                           bump=c_gp)

        qstack = ExitStack()
        Qf = qstack.enter_context(nc.sbuf_tensor("Qf", [128, NH * T], BF16))
        OHf = qstack.enter_context(nc.sbuf_tensor("OHf", [128, NH * T], BF16))

        att = ExitStack()
        with att:
            xT = att.enter_context(nc.sbuf_tensor("xT", [128, 16 * 2048], BF16))
            Mb = att.enter_context(nc.sbuf_tensor("Mb", [128, 2 * 4096], BF16))
            for qc in range(2):
                P.emit("gpsimd",
                       (lambda qc_=qc: (lambda g: g.dma_start(
                           out=Mb[:, qc_ * 4096:(qc_ + 1) * 4096],
                           in_=Mq[qc_])))(),
                       bump=dch["mld"], amt=16)
            w_mask_ld = (dch["mld"], dch["mld"].count)

            w_xT_ready = None
            w_xqT_ready = None
            w_Qf_ready = None
            w_Q_lastmm = None
            psc = ExitStack()
            with psc:
                psT = [psc.enter_context(nc.psum_tensor("psT%d" % i,
                                                        [128, 1024], BF16))
                       for i in range(2)]
                xbq = ExitStack()
                with xbq:
                    xb = [xbq.enter_context(nc.sbuf_tensor(
                        "xb%d" % i, [128, D], BF16)) for i in range(2)]
                    w_xb_use = [None, None]
                    w_psT_evac = [None, None]

                    def build_T(src_ap, dst, kt_idx, ncols, parts):
                        s = kt_idx % 2
                        ch = dch["xb%d" % s]
                        P.emit("gpsimd",
                               (lambda a=src_ap, s_=s:
                                (lambda g: g.dma_start(out=xb[s_][:], in_=a)))(),
                               waits=[w_xb_use[s]], bump=ch, amt=16)
                        w_dma = (ch, ch.count)
                        dst3 = dst.rearrange("p (dc k) -> p dc k", k=ncols)
                        for half in range(2):
                            b = half
                            last = None
                            for j in range(8):
                                dc = half * 8 + j
                                last = P.emit(
                                    "tensor",
                                    (lambda b_=b, j_=j, s_=s, dc_=dc:
                                     (lambda t: t.transpose(
                                         psT[b_][:, j_ * 128:(j_ + 1) * 128],
                                         xb[s_][:, dc_ * 128:(dc_ + 1) * 128],
                                         ident[:])))(),
                                    waits=[w_dma, w_ident, w_psT_evac[b]],
                                    bump=(c_pe if j == 7 else None))
                            w_xb_use[s] = last
                            ov = dst3[:, half * 8:half * 8 + 8,
                                      kt_idx * 128:kt_idx * 128 + 128]
                            iv = psT[b][:, 0:1024].rearrange(
                                "p (dc k) -> p dc k", k=128)
                            w_psT_evac[b] = P.emit(
                                "vector",
                                (lambda o=ov, i=iv:
                                 (lambda v: v.tensor_copy(o, i)))(),
                                waits=[last], bump=c_ve)
                            parts.append(w_psT_evac[b])

                    pts = []
                    for kt in range(KT):
                        build_T(xk[kt * 128:(kt + 1) * 128, :], xT[:], kt,
                                2048, pts)
                    w_xT_ready = pts[-1]
                    pts2 = []
                    xqT = xbq.enter_context(
                        nc.sbuf_tensor("xqT", [128, 16 * T], BF16))
                    for qt in range(NT):
                        build_T(xq[qt * 128:(qt + 1) * 128, :], xqT[:], qt,
                                T, pts2)
                    w_xqT_ready = pts2[-1]

                    wqb = [xbq.enter_context(nc.sbuf_tensor(
                        "wqb%d" % i, [128, 16 * 128], BF16)) for i in range(2)]
                    psq = ExitStack()
                    with psq:
                        psQ = [psq.enter_context(nc.psum_tensor(
                            "psQ%d" % i, [128, 512], F32)) for i in range(2)]
                        w_wqb_use = [None, None]
                        w_psQ_evac = [None, None]
                        for h in range(NH):
                            s = h % 2
                            ch = dch["wqb%d" % s]
                            P.emit("gpsimd",
                                   (lambda s_=s, h_=h: (lambda g: g.dma_start(
                                       out=wqb[s_][:].rearrange(
                                           "p (c j) -> p c j", c=16),
                                       in_=Wq[h_].rearrange(
                                           "(c p) j -> p c j", p=128))))(),
                                   waits=[w_wqb_use[s]], bump=ch, amt=16)
                            w_dma = (ch, ch.count)
                            lastmm = None
                            for d in range(16):
                                for q2 in range(2):
                                    lastmm = P.emit(
                                        "tensor",
                                        (lambda s_=s, d_=d, q2_=q2:
                                         (lambda t: t.matmul(
                                             psQ[q2_][:],
                                             wqb[s_][:, d_ * 128:
                                                     (d_ + 1) * 128],
                                             xqT[:, d_ * T + q2_ * 512:
                                                 d_ * T + q2_ * 512 + 512],
                                             start=(d_ == 0),
                                             stop=(d_ == 15))))(),
                                        waits=[w_dma, w_xqT_ready,
                                               w_psQ_evac[q2]],
                                        bump=(c_pe if d == 15 else None))
                            w_wqb_use[s] = lastmm
                            w_Q_lastmm = lastmm
                            for q2 in range(2):
                                w_psQ_evac[q2] = P.emit(
                                    "scalar",
                                    (lambda h_=h, q2_=q2: (lambda sc: sc.mul(
                                        Qf[:, h_ * T + q2_ * 512:
                                           h_ * T + q2_ * 512 + 512],
                                        psQ[q2_][:], cst[:, 3:4])))(),
                                    waits=[lastmm, w_cst], bump=c_ae)
                        w_Qf_ready = w_psQ_evac[1]

            Kf2 = [att.enter_context(nc.sbuf_tensor("Kf%d" % i,
                                                     [128, 2048], BF16))
                   for i in range(2)]
            Vtok = att.enter_context(nc.sbuf_tensor("Vtok", [128, KT * 512],
                                                    BF16))
            PT = att.enter_context(nc.sbuf_tensor("PT", [128, 8 * 512], BF16))
            rZb = att.enter_context(nc.sbuf_tensor("rZb", [128, 512], F32))
            wkb = [att.enter_context(nc.sbuf_tensor("wkb%d" % i,
                                                    [128, 16 * 128], BF16))
                   for i in range(2)]
            wvb = att.enter_context(nc.sbuf_tensor("wvb", [128, 16 * 512],
                                                   BF16))

            bc_bar = [w_Q_lastmm, w_Qf_ready, w_xqT_ready]

            psa = ExitStack()
            with psa:
                psP = [psa.enter_context(nc.psum_tensor("psP%d" % i,
                                                        [128, 512], F32))
                       for i in range(2)]
                psS = psa.enter_context(nc.psum_tensor("psS", [128, 2048],
                                                       F32))
                psZ = psa.enter_context(nc.psum_tensor("psZ", [128, 512], F32))
                psO = psa.enter_context(nc.psum_tensor("psO", [128, 512], F32))

                w_wkb_use = [w_Q_lastmm, w_Q_lastmm]
                w_wvb_use = w_Q_lastmm
                w_mask_last = [None]
                w_xT_last = [None]
                w_psP_evac = [w_Qf_ready, w_Qf_ready]
                w_Vtok_ready = None
                w_recip_prev = None
                w_oh_prev = None
                w_oh_all = []
                w_Kf_ready = [None, None]
                wkb_dma = [None] * (NH + 2)

                def emit_wkb_load(h_):
                    s_ = h_ % 2
                    chk = dch["wkb%d" % s_]
                    P.emit("gpsimd",
                           (lambda s__=s_, h__=h_: (lambda g: g.dma_start(
                               out=wkb[s__][:].rearrange("p (c j) -> p c j",
                                                         c=16),
                               in_=Wk[h__].rearrange("(c p) j -> p c j",
                                                     p=128))))(),
                           waits=[w_wkb_use[s_]] + bc_bar, bump=chk, amt=16)
                    wkb_dma[h_] = (chk, chk.count)

                def kproj_chunk(h_, kc):
                    s_ = h_ % 2
                    b = kc % 2
                    lastmm = None
                    for d in range(16):
                        lastmm = P.emit(
                            "tensor",
                            (lambda s__=s_, b_=b, d_=d, kc_=kc:
                             (lambda t: t.matmul(
                                 psP[b_][:],
                                 wkb[s__][:, d_ * 128:(d_ + 1) * 128],
                                 xT[:, d_ * 2048 + kc_ * 512:
                                    d_ * 2048 + kc_ * 512 + 512],
                                 start=(d_ == 0), stop=(d_ == 15))))(),
                            waits=[wkb_dma[h_], w_psP_evac[b], w_xT_ready],
                            bump=(c_pe if d == 15 else None))
                    w_psP_evac[b] = P.emit(
                        "scalar",
                        (lambda s__=s_, kc_=kc, b_=b: (lambda sc: sc.copy(
                            Kf2[s__][:, kc_ * 512:(kc_ + 1) * 512],
                            psP[b_][:])))(),
                        waits=[lastmm], bump=c_ae)
                    w_wkb_use[s_] = lastmm
                    w_xT_last[0] = lastmm
                    if kc == 3:
                        w_Kf_ready[s_] = w_psP_evac[b]

                emit_wkb_load(0)
                emit_wkb_load(1)
                for kc in range(4):
                    kproj_chunk(0, kc)
                kproj_chunk(1, 0)
                kproj_chunk(1, 1)

                def emit_wvb_load(grp_):
                    P.emit("gpsimd",
                           (lambda g_=grp_: (lambda g: g.dma_start(
                               out=wvb[:].rearrange("p (c j) -> p c j",
                                                    c=16),
                               in_=Wv4[g_].rearrange(
                                   "(c p) j -> p c j", p=128))))(),
                           waits=[w_wvb_use] + bc_bar,
                           bump=dch["wvb"], amt=16)
                    wvb_dma[0] = (dch["wvb"], dch["wvb"].count)

                wvb_dma = [None]
                emit_wvb_load(0)
                for h in range(NH):
                    if h + 2 < NH:
                        emit_wkb_load(h + 2)
                    if h % 4 == 2 and (h + 2) // 4 < 4:
                        emit_wvb_load((h + 2) // 4)
                    if h % 4 == 0:
                        w_dma = wvb_dma[0]
                        vts = []
                        lastmm = None
                        for kt in range(KT):
                            b = kt % 2
                            for d in range(16):
                                lastmm = P.emit(
                                    "tensor",
                                    (lambda b_=b, d_=d, kt_=kt:
                                     (lambda t: t.matmul(
                                         psP[b_][:],
                                         xT[:, d_ * 2048 + kt_ * 128:
                                            d_ * 2048 + kt_ * 128 + 128],
                                         wvb[:, d_ * 512:(d_ + 1) * 512],
                                         start=(d_ == 0), stop=(d_ == 15))))(),
                                    waits=[w_dma, w_psP_evac[b], w_xT_ready],
                                    bump=(c_pe if d == 15 else None))
                            w_psP_evac[b] = P.emit(
                                "vector",
                                (lambda kt_=kt, b_=b: (lambda v: v.tensor_copy(
                                    Vtok[:, kt_ * 512:(kt_ + 1) * 512],
                                    psP[b_][:])))(),
                                waits=[lastmm], bump=c_ve)
                            vts.append(w_psP_evac[b])
                        w_wvb_use = lastmm
                        w_xT_last[0] = lastmm
                        w_Vtok_ready = vts[-1]

                    if h + 1 >= NH:
                        pend = []
                    elif h == 0:
                        pend = [2, 3]
                    else:
                        pend = list(range(4))

                    def inject(n):
                        for _ in range(n):
                            if pend:
                                kproj_chunk(h + 1, pend.pop(0))

                    for qc in range(2):
                        kts = QC_KTS[qc]
                        qoff = qc * 512
                        w_Z_stop = None
                        w_O_stop = None
                        for ktg in range(0, kts, 8):
                            exp_toks = []
                            for kt2 in range(ktg, ktg + 8, 2):
                                half = (kt2 // 2) % 2
                                st_last = None
                                for kt in (kt2, kt2 + 1):
                                    col = half * 1024 + (kt % 2) * 512
                                    st_last = P.emit(
                                        "tensor",
                                        (lambda kt_=kt, h_=h, qoff_=qoff,
                                                col_=col:
                                         (lambda t: t.matmul(
                                             psS[:, col_:col_ + 512],
                                             Kf2[h_ % 2][:, kt_ * 128:
                                                         (kt_ + 1) * 128],
                                             Qf[:, h_ * T + qoff_:
                                                h_ * T + qoff_ + 512],
                                             start=True, stop=True)))(),
                                        waits=[w_Kf_ready[h % 2], w_Qf_ready,
                                               exp_toks[-2] if
                                               len(exp_toks) >= 2 else None,
                                               w_recip_prev, w_oh_prev],
                                        bump=(c_pe if kt == kt2 + 1
                                              else None))
                                tok_e = P.emit(
                                    "scalar",
                                    (lambda kt2_=kt2, ktg_=ktg, half_=half:
                                     (lambda sc: sc.activation(
                                         PT[:, (kt2_ - ktg_) * 512:
                                            (kt2_ - ktg_) * 512 + 1024],
                                         psS[:, half_ * 1024:
                                             half_ * 1024 + 1024],
                                         AF.Exp, bias=cst[:, 0:1],
                                         scale=1.0)))(),
                                    waits=[st_last, w_cst], bump=c_ae)
                                exp_toks.append(tok_e)
                            inject(2 if (qc == 0) else 1)
                            masked = (qc == 0 and ktg == 0) or \
                                     (qc == 1 and ktg == 8)
                            if masked:
                                mq = 0 if qc == 0 else 1
                                w_mask = P.emit(
                                    "vector",
                                    (lambda mq_=mq: (lambda v: v.tensor_mul(
                                        PT[:, 0:4096], PT[:, 0:4096],
                                        Mb[:, mq_ * 4096:
                                           (mq_ + 1) * 4096])))(),
                                    waits=[exp_toks[-1], w_mask_ld],
                                    bump=c_ve)
                                w_mask_last[0] = w_mask
                                pt_wait = [w_mask]
                            else:
                                pt_wait = [exp_toks[-1]]
                            for kt in range(ktg, ktg + 8):
                                w_Z_stop = P.emit(
                                    "tensor",
                                    (lambda kt_=kt, ktg_=ktg, kts_=kts:
                                     (lambda t: t.matmul(
                                         psZ[:], ones[:],
                                         PT[:, (kt_ - ktg_) * 512:
                                            (kt_ - ktg_) * 512 + 512],
                                         start=(kt_ == 0),
                                         stop=(kt_ == kts_ - 1))))(),
                                    waits=pt_wait + [w_ones, w_recip_prev],
                                    bump=(c_pe if kt == kts - 1 else None))
                            for kt in range(ktg, ktg + 8):
                                w_O_stop = P.emit(
                                    "tensor",
                                    (lambda kt_=kt, ktg_=ktg, kts_=kts, h_=h:
                                     (lambda t: t.matmul(
                                         psO[:],
                                         Vtok[:, kt_ * 512 + (h_ % 4) * 128:
                                              kt_ * 512 + (h_ % 4) * 128
                                              + 128],
                                         PT[:, (kt_ - ktg_) * 512:
                                            (kt_ - ktg_) * 512 + 512],
                                         start=(kt_ == 0),
                                         stop=(kt_ == kts_ - 1))))(),
                                    waits=pt_wait + [w_Vtok_ready, w_oh_prev],
                                    bump=(c_pe if kt == kts - 1 else None))
                        w_recip_prev = P.emit(
                            "vector",
                            lambda v: v.reciprocal(rZb[:], psZ[:]),
                            waits=[w_Z_stop], bump=c_ve)
                        w_oh_prev = P.emit(
                            "vector",
                            (lambda h_=h, qoff_=qoff: (lambda v: v.tensor_mul(
                                OHf[:, h_ * T + qoff_:h_ * T + qoff_ + 512],
                                psO[:], rZb[:])))(),
                            waits=[w_O_stop, w_recip_prev], bump=c_ve)
                        w_oh_all.append(w_oh_prev)
                    inject(4)
                w_OHf_ready = w_oh_all[-1]

            if "qf" in dbg_out:
                P.emit("gpsimd", lambda g: g.dma_start(
                    out=dbg_out["qf"], in_=Qf[:]),
                    waits=[w_Qf_ready], bump=c_so, amt=16)
            if "kf" in dbg_out:
                P.emit("gpsimd", lambda g: g.dma_start(
                    out=dbg_out["kf"], in_=Kf[:]),
                    waits=[(c_ae, c_ae.count)], bump=c_so, amt=16)
            if "vt" in dbg_out:
                P.emit("gpsimd", lambda g: g.dma_start(
                    out=dbg_out["vt"], in_=Vtok[:]),
                    waits=[(c_ve, c_ve.count)], bump=c_so, amt=16)
            if "oh" in dbg_out:
                P.emit("gpsimd", lambda g: g.dma_start(
                    out=dbg_out["oh"], in_=OHf[:]),
                    waits=[w_OHf_ready], bump=c_so, amt=16)

        acc = es.enter_context(nc.sbuf_tensor("acc", [128, NT * D], F32,
                                              side="right"))
        x1T = es.enter_context(nc.sbuf_tensor("x1T", [128, 16 * T], BF16,
                                              side="right"))
        xb2 = [es.enter_context(nc.sbuf_tensor("xb2_%d" % i, [128, D], BF16,
                                               side="right"))
               for i in range(2)]
        cw = es.enter_context(nc.sbuf_tensor("cw", [128, NT * NE], F32,
                                             side="right"))
        sva = es.enter_context(nc.sbuf_tensor("sva", [128, NT * 8], F32,
                                              side="right"))
        gscA = es.enter_context(nc.sbuf_tensor("gscA", [128, NT * 64], F32,
                                               side="right"))
        gscB = es.enter_context(nc.sbuf_tensor("gscB", [128, NT * 64], F32,
                                               side="right"))

        def A_(t_, a, b=None):
            if b is None:
                b = a + 1
            return gscA[:, t_ * 64 + a:t_ * 64 + b]

        def B_(t_, a, b=None):
            if b is None:
                b = a + 1
            return gscB[:, t_ * 64 + a:t_ * 64 + b]

        def SV(t_, j):
            return sva[:, t_ * 8 + j:t_ * 8 + j + 1]

        w_gwb = P.emit(
            "gpsimd",
            lambda g: g.dma_start(
                out=gwb[:].rearrange("p (c j) -> p c j", c=16),
                in_=Gw.rearrange("(c p) j -> p c j", p=128)),
            bump=dch["gwb"], amt=16)

        d_bar = bar()
        w_ae_bar = (c_ae, c_ae.count)
        for t_ in range(NT):
            P.emit("sync",
                   (lambda t__=t_: (lambda sy: sy.dma_start(
                       out=acc[:, t__ * D:(t__ + 1) * D],
                       in_=xq[t__ * 128:(t__ + 1) * 128, :])))(),
                   waits=d_bar, bump=c_sd, amt=16)
        w_acc_ld = (c_sd, c_sd.count)

        w_acc_done = [None] * NT
        w_xb2_rd = [None, None]
        w_xb2_wr = [None, None]
        w_psTD_evac = [None, None]

        def ln_stats(w_ready, w_pre, pbase):
            s1t, s2t, t1, t2, t3 = [], [], [], [], []
            for t_ in range(NT):
                s1t.append(P.emit(
                    "vector",
                    (lambda t__=t_: (lambda v: v.reduce_sum(
                        SV(t__, 0),
                        gscA[:, t__ * 64 + pbase:t__ * 64 + pbase + 4],
                        axis=AX.X)))(),
                    waits=[w_ready[t_], w_pre], bump=c_ve))
            for t_ in range(NT):
                s2t.append(P.emit(
                    "scalar",
                    (lambda t__=t_: (lambda sc: sc.activation(
                        xb2[t__ % 2][:], acc[:, t__ * D:(t__ + 1) * D],
                        AF.Square, accum_out=SV(t__, 1))))(),
                    waits=[w_ready[t_], w_pre, w_xb2_rd[t_ % 2],
                           w_xb2_wr[t_ % 2]],
                    bump=c_ae))
                w_xb2_wr[t_ % 2] = s2t[-1]
            for t_ in range(NT):
                t1.append(P.emit(
                    "vector",
                    (lambda t__=t_: (lambda v: v.tensor_scalar(
                        SV(t__, 2), SV(t__, 0), -1.0 / D, None,
                        op0=ALU.mult)))(),
                    waits=[s1t[t_]], bump=c_ve))
            for t_ in range(NT):
                t1[t_] = P.emit(
                    "vector",
                    (lambda t__=t_: (lambda v: v.tensor_mul(
                        SV(t__, 3), SV(t__, 2), SV(t__, 2))))(),
                    waits=[t1[t_]], bump=c_ve)
            for t_ in range(NT):
                t2.append(P.emit(
                    "vector",
                    (lambda t__=t_: (lambda v: v.scalar_tensor_tensor(
                        SV(t__, 4), SV(t__, 1), 1.0 / D, SV(t__, 3),
                        op0=ALU.mult, op1=ALU.subtract)))(),
                    waits=[t1[t_], s2t[t_]], bump=c_ve))
            for t_ in range(NT):
                t2[t_] = P.emit(
                    "scalar",
                    (lambda t__=t_: (lambda sc: sc.activation(
                        SV(t__, 5), SV(t__, 4), AF.Ln,
                        bias=cst[:, 1:2], scale=1.0)))(),
                    waits=[t2[t_], w_cst], bump=c_ae)
            for t_ in range(NT):
                t2[t_] = P.emit(
                    "scalar",
                    (lambda t__=t_: (lambda sc: sc.activation(
                        SV(t__, 5), SV(t__, 5), AF.Exp,
                        bias=0.0, scale=cst[:, 2:3])))(),
                    waits=[t2[t_]], bump=c_ae)
            for t_ in range(NT):
                t3.append(P.emit(
                    "vector",
                    (lambda t__=t_: (lambda v: v.tensor_scalar(
                        acc[:, t__ * D:(t__ + 1) * D],
                        acc[:, t__ * D:(t__ + 1) * D],
                        SV(t__, 2), SV(t__, 5),
                        op0=ALU.add, op1=ALU.mult)))(),
                    waits=[t2[t_]], bump=c_ve))
            return t3

        dsc = ExitStack()
        with dsc:
            wob = [dsc.enter_context(nc.sbuf_tensor("wob%d" % i,
                                                    [128, 16 * 512], BF16))
                   for i in range(2)]
            psd = ExitStack()
            with psd:
                psD2 = [psd.enter_context(nc.psum_tensor("psD2%d" % i,
                                                         [128, 512], F32))
                        for i in range(2)]
                psTD = [psd.enter_context(nc.psum_tensor("psTD%d" % i,
                                                         [128, 1024], BF16))
                        for i in range(2)]
                psG = [psd.enter_context(nc.psum_tensor("psG%d" % i,
                                                        [128, 16], F32))
                       for i in range(2)]

                w_wob_use = [[w_xT_last[0]], [w_xT_last[0]]]
                w_psD2_evac = [None, None]
                first_mm = True
                for oc in range(d_oc):
                    s = oc % 2
                    ch = dch["wob%d" % s]
                    P.emit("gpsimd",
                           (lambda s_=s, oc_=oc: (lambda g: g.dma_start(
                               out=wob[s_][:].rearrange("p (c j) -> p c j",
                                                        c=16),
                               in_=Wo.rearrange("(c p) o -> p c o",
                                                p=128)[:, :, oc_ * 512:
                                                       (oc_ + 1) * 512])))(),
                           waits=list(w_wob_use[s]), bump=ch, amt=16)
                    w_dma = (ch, ch.count)
                    for t_ in range(NT):
                        b = t_ % 2
                        lastmm = None
                        for hc in range(16):
                            lastmm = P.emit(
                                "tensor",
                                (lambda s_=s, b_=b, hc_=hc, t__=t_:
                                 (lambda t: t.matmul(
                                     psD2[b_][:],
                                     OHf[:, hc_ * T + t__ * 128:
                                         hc_ * T + t__ * 128 + 128],
                                     wob[s_][:, hc_ * 512:(hc_ + 1) * 512],
                                     start=(hc_ == 0), stop=(hc_ == 15))))(),
                                waits=[w_dma, w_OHf_ready, w_psD2_evac[b]]
                                + ([w_ae_bar] if first_mm else []),
                                bump=(c_pe if hc == 15 else None))
                            first_mm = False
                        w_wob_use[s] = [lastmm]
                        w_psD2_evac[b] = P.emit(
                            "vector",
                            (lambda t__=t_, oc_=oc, b_=b:
                             (lambda v: v.scalar_tensor_tensor(
                                 acc[:, t__ * D + oc_ * 512:
                                     t__ * D + oc_ * 512 + 512],
                                 psD2[b_][:], 1.0,
                                 acc[:, t__ * D + oc_ * 512:
                                     t__ * D + oc_ * 512 + 512],
                                 op0=ALU.mult, op1=ALU.add,
                                 accum_out=gscA[:, t__ * 64 + 56 + oc_:
                                                t__ * 64 + 57 + oc_])))(),
                            waits=[lastmm, w_acc_ld], bump=c_ve)
                        if oc == d_oc - 1:
                            w_acc_done[t_] = w_psD2_evac[b]

                w_dump = None
                if "accD" in dbg_out:
                    for t_ in range(NT):
                        w_dump = P.emit(
                            "sync",
                            (lambda t__=t_: (lambda sy: sy.dma_start(
                                out=dbg_out["accD"][t__ * 128:
                                                    (t__ + 1) * 128, :],
                                in_=acc[:, t__ * D:(t__ + 1) * D])))(),
                            waits=[w_acc_done[t_]], bump=c_so, amt=16)

                w_x1_tiles = ln_stats(w_acc_done, w_dump, 56)
                w_x1_ready = w_x1_tiles[-1]

                x1T3 = x1T[:].rearrange("p (dc k) -> p dc k", k=T)
                w_psG_rd = [[None, None], [None, None]]
                w_x1T_evac = [None] * NT
                w_cw_tiles = [None] * NT
                w_gsA_init = [None] * NT
                for t_ in range(NT):
                    w_gsA_init[t_] = P.emit(
                        "vector",
                        (lambda t__=t_: (lambda v: v.memset(
                            A_(t__, 20, 24), -1e9)))(),
                        waits=d_bar, bump=c_ve)

                def gate_stage(pair, fn_t, eng="vector", waits_t=None,
                               bump=c_ve):
                    out = {}
                    for t_ in pair:
                        out[t_] = P.emit(eng, fn_t(t_),
                                         waits=(waits_t(t_) if waits_t
                                                else []),
                                         bump=bump)
                    return out

                for p0 in range(0, NT, 2):
                    pair = (p0, p0 + 1)
                    for t_ in pair:
                        w_cast = P.emit(
                            "scalar",
                            (lambda t__=t_: (lambda sc: sc.copy(
                                xb2[t__ % 2][:],
                                acc[:, t__ * D:(t__ + 1) * D])))(),
                            waits=[w_x1_tiles[t_], w_xb2_rd[t_ % 2],
                                   w_xb2_wr[t_ % 2]],
                            bump=c_ae)
                        w_xb2_wr[t_ % 2] = w_cast
                        for half in range(2):
                            b = half
                            last = None
                            for j in range(8):
                                dc = half * 8 + j
                                last = P.emit(
                                    "tensor",
                                    (lambda b_=b, j_=j, dc_=dc, t__=t_:
                                     (lambda t: t.transpose(
                                         psTD[b_][:, j_ * 128:(j_ + 1) * 128],
                                         xb2[t__ % 2][:, dc_ * 128:
                                                      (dc_ + 1) * 128],
                                         ident[:])))(),
                                    waits=[w_cast, w_psTD_evac[b], w_ident],
                                    bump=(c_pe if j == 7 else None))
                            w_xb2_rd[t_ % 2] = last
                            ov = x1T3[:, half * 8:half * 8 + 8,
                                      t_ * 128:t_ * 128 + 128]
                            iv = psTD[b][:, 0:1024].rearrange(
                                "p (dc k) -> p dc k", k=128)
                            w_psTD_evac[b] = P.emit(
                                "vector",
                                (lambda o=ov, i=iv:
                                 (lambda v: v.tensor_copy(o, i)))(),
                                waits=[last], bump=c_ve)
                        w_x1T_evac[t_] = w_psTD_evac[1]
                    w_gmm = {}
                    for t_ in pair:
                        lastg = None
                        for d in range(16):
                            lastg = P.emit(
                                "tensor",
                                (lambda d_=d, t__=t_: (lambda t: t.matmul(
                                    psG[t__ % 2][:],
                                    x1T[:, d_ * T + t__ * 128:
                                        d_ * T + t__ * 128 + 128],
                                    gwb[:, d_ * 16:(d_ + 1) * 16],
                                    start=(d_ == 0), stop=(d_ == 15))))(),
                                waits=[w_gwb, w_x1T_evac[t_],
                                       w_psG_rd[t_ % 2][0],
                                       w_psG_rd[t_ % 2][1]],
                                bump=(c_pe if d == 15 else None))
                        w_gmm[t_] = lastg
                    tk = gate_stage(pair, lambda t_: (
                        lambda t__=t_: (lambda v: v.reduce_max(
                            A_(t__, 52), psG[t__ % 2][:], axis=AX.X,
                            negate=True)))(),
                        waits_t=lambda t_: [w_gmm[t_]])
                    for t_ in pair:
                        w_psG_rd[t_ % 2][0] = tk[t_]
                    tke = {}
                    for t_ in pair:
                        tke[t_] = P.emit(
                            "scalar",
                            (lambda t__=t_: (lambda sc: sc.activation(
                                A_(t__, 0, 16), psG[t__ % 2][:], AF.Exp,
                                bias=A_(t__, 52), scale=1.0,
                                accum_out=A_(t__, 53))))(),
                            waits=[tk[t_]], bump=c_ae)
                        w_psG_rd[t_ % 2][1] = tke[t_]
                    tk = gate_stage(pair, lambda t_: (
                        lambda t__=t_: (lambda v: v.reciprocal(
                            A_(t__, 54), A_(t__, 53))))(),
                        waits_t=lambda t_: [tke[t_]])
                    tk = gate_stage(pair, lambda t_: (
                        lambda t__=t_: (lambda v: v.tensor_single_scalar(
                            A_(t__, 0, 16), A_(t__, 0, 16), A_(t__, 54),
                            ALU.mult)))(),
                        waits_t=lambda t_: [tk[t_]])
                    for j in range(4):
                        tk = gate_stage(pair, lambda t_: (
                            lambda t__=t_, j_=j: (lambda v: v.reduce_max(
                                A_(t__, 16 + j_), A_(t__, 4 * j_, 4 * j_ + 4),
                                axis=AX.X)))(),
                            waits_t=lambda t_: [tk[t_]])
                    tk = gate_stage(pair, lambda t_: (
                        lambda t__=t_: (lambda v: v.max(
                            A_(t__, 24, 32), A_(t__, 16, 24))))(),
                        waits_t=lambda t_: [tk[t_], w_gsA_init[t_]])
                    tk = gate_stage(pair, lambda t_: (
                        lambda t__=t_: (lambda v: v.tensor_single_scalar(
                            A_(t__, 32, 36), A_(t__, 16, 20), A_(t__, 26),
                            ALU.is_gt)))(),
                        waits_t=lambda t_: [tk[t_]])
                    for j in range(4):
                        tk = gate_stage(pair, lambda t_: (
                            lambda t__=t_, j_=j:
                            (lambda v: v.tensor_single_scalar(
                                A_(t__, 36 + 4 * j_, 40 + 4 * j_), zz4[:],
                                A_(t__, 32 + j_), ALU.add)))(),
                            waits_t=lambda t_: [tk[t_], w_zz4])
                    tk = gate_stage(pair, lambda t_: (
                        lambda t__=t_: (lambda v: v.tensor_mul(
                            B_(t__, 0, 16), A_(t__, 0, 16),
                            A_(t__, 36, 52))))(),
                        waits_t=lambda t_: [tk[t_]])
                    tk = gate_stage(pair, lambda t_: (
                        lambda t__=t_: (lambda v: v.max(
                            B_(t__, 16, 24), B_(t__, 0, 16))))(),
                        waits_t=lambda t_: [tk[t_]])
                    tk = gate_stage(pair, lambda t_: (
                        lambda t__=t_: (lambda v: v.tensor_single_scalar(
                            B_(t__, 24, 40), B_(t__, 0, 16), B_(t__, 20),
                            ALU.is_gt)))(),
                        waits_t=lambda t_: [tk[t_]])
                    tk = gate_stage(pair, lambda t_: (
                        lambda t__=t_: (lambda v: v.tensor_mul(
                            B_(t__, 40, 56), B_(t__, 0, 16),
                            B_(t__, 24, 40))))(),
                        waits_t=lambda t_: [tk[t_]])
                    tk = gate_stage(pair, lambda t_: (
                        lambda t__=t_: (lambda v: v.reduce_sum(
                            B_(t__, 56), B_(t__, 40, 56), axis=AX.X)))(),
                        waits_t=lambda t_: [tk[t_]])
                    tk = gate_stage(pair, lambda t_: (
                        lambda t__=t_: (lambda v: v.tensor_single_scalar(
                            B_(t__, 56), B_(t__, 56), 1e-20, ALU.add)))(),
                        waits_t=lambda t_: [tk[t_]])
                    tk = gate_stage(pair, lambda t_: (
                        lambda t__=t_: (lambda v: v.reciprocal(
                            B_(t__, 56), B_(t__, 56))))(),
                        waits_t=lambda t_: [tk[t_]])
                    tk = gate_stage(pair, lambda t_: (
                        lambda t__=t_: (lambda v: v.tensor_single_scalar(
                            cw[:, t__ * NE:t__ * NE + 16], B_(t__, 40, 56),
                            B_(t__, 56), ALU.mult)))(),
                        waits_t=lambda t_: [tk[t_]])
                    for t_ in pair:
                        w_cw_tiles[t_] = tk[t_]

                w_cw_ready = P.emit(
                    "vector",
                    lambda v: v.memset(
                        cw[:].rearrange("p (t e) -> p t e",
                                        e=NE)[:, :, 16:18], 1.0),
                    waits=[w_cw_tiles[-1]], bump=c_ve)
                w_x1T_ready = w_psTD_evac[1]
                w_gate_last_ve = w_cw_ready
                w_gate_last_ae = tke[pair[-1]]

            if "x1" in dbg_out:
                for t_ in range(NT):
                    P.emit("sync",
                           (lambda t__=t_: (lambda sy: sy.dma_start(
                               out=dbg_out["x1"][t__ * 128:(t__ + 1) * 128, :],
                               in_=acc[:, t__ * D:(t__ + 1) * D])))(),
                           waits=[w_x1_ready], bump=c_so, amt=16)
            if "cw" in dbg_out:
                P.emit("sync", lambda sy: sy.dma_start(
                    out=dbg_out["cw"], in_=cw[:]),
                    waits=[w_cw_ready], bump=c_so, amt=16)
        qstack.close()

        e_bar = [(c_pe, c_pe.count)]
        msc = ExitStack()
        with msc:
            upg = [msc.enter_context(nc.sbuf_tensor("upg%d" % i,
                                                    [128, 16 * 128], BF16))
                   for i in range(2)]
            upu = [msc.enter_context(nc.sbuf_tensor("upu%d" % i,
                                                    [128, 16 * 128], BF16))
                   for i in range(2)]
            dnb = [msc.enter_context(nc.sbuf_tensor("dnb%d" % i,
                                                    [128, 11 * 512], BF16))
                   for i in range(4)]
            sgT = msc.enter_context(nc.sbuf_tensor("sgT", [128, T], BF16))
            apT = msc.enter_context(nc.sbuf_tensor("apT", [128, NPAIR * T],
                                                   BF16))
            pse = ExitStack()
            with pse:
                psUg = pse.enter_context(nc.psum_tensor("psUg", [128, T],
                                                        F32))
                psUu = pse.enter_context(nc.psum_tensor("psUu", [128, T],
                                                        F32))
                psDn = [pse.enter_context(nc.psum_tensor("psDn%d" % i,
                                                         [128, 512], F32))
                        for i in range(2)]

                w_upg_use = [None, None]
                w_upu_use = [None, None]
                w_dnb_use = [None, None, None, None]
                w_silu = w_gate_last_ae
                w_ap_prev = w_gate_last_ve
                w_dn_evac = [None, None]
                w_acc_final = [None] * NT

                for e in range(n_experts):
                    up_ap, g_off, u_off, dn_ap, cwcol = etab(e)
                    up3 = up_ap.rearrange("(c p) o -> p c o", p=128)
                    dn3 = dn_ap.rearrange("(ic p) o -> p ic o", p=128)
                    w_apm = [None] * NPAIR
                    for i in range(NPAIR):
                        s = i % 2
                        chg = dch["upg%d" % s]
                        chu = dch["upu%d" % s]
                        P.emit("gpsimd",
                               (lambda s_=s, off=g_off + i * 128, u3=up3:
                                (lambda g: g.dma_start(
                                    out=upg[s_][:].rearrange(
                                        "p (c j) -> p c j", c=16),
                                    in_=u3[:, :, off:off + 128])))(),
                               waits=[w_upg_use[s]] + e_bar,
                               bump=chg, amt=16)
                        w_g = (chg, chg.count)
                        P.emit("gpsimd",
                               (lambda s_=s, off=u_off + i * 128, u3=up3:
                                (lambda g: g.dma_start(
                                    out=upu[s_][:].rearrange(
                                        "p (c j) -> p c j", c=16),
                                    in_=u3[:, :, off:off + 128])))(),
                               waits=[w_upu_use[s]] + e_bar,
                               bump=chu, amt=16)
                        w_u = (chu, chu.count)
                        glast = None
                        for d in range(16):
                            for q2 in range(2):
                                glast = P.emit(
                                    "tensor",
                                    (lambda s_=s, d_=d, q2_=q2:
                                     (lambda t: t.matmul(
                                         psUg[:, q2_ * 512:q2_ * 512 + 512],
                                         upg[s_][:, d_ * 128:(d_ + 1) * 128],
                                         x1T[:, d_ * T + q2_ * 512:
                                             d_ * T + q2_ * 512 + 512],
                                         start=(d_ == 0), stop=(d_ == 15))))(),
                                    waits=[w_g, w_x1T_ready, w_silu],
                                    bump=(c_pe if (d == 15 and q2 == 1)
                                          else None))
                        w_upg_use[s] = glast
                        ulast = None
                        for d in range(16):
                            for q2 in range(2):
                                ulast = P.emit(
                                    "tensor",
                                    (lambda s_=s, d_=d, q2_=q2:
                                     (lambda t: t.matmul(
                                         psUu[:, q2_ * 512:q2_ * 512 + 512],
                                         upu[s_][:, d_ * 128:(d_ + 1) * 128],
                                         x1T[:, d_ * T + q2_ * 512:
                                             d_ * T + q2_ * 512 + 512],
                                         start=(d_ == 0), stop=(d_ == 15))))(),
                                    waits=[w_u, w_ap_prev, w_x1T_ready],
                                    bump=(c_pe if (d == 15 and q2 == 1)
                                          else None))
                        w_upu_use[s] = ulast
                        w_silu = P.emit(
                            "scalar",
                            lambda sc: sc.activation(sgT[:], psUg[:],
                                                     AF.Silu),
                            waits=[glast, w_ap_prev], bump=c_ae)
                        w_ap_prev = P.emit(
                            "vector",
                            (lambda i_=i: (lambda v: v.tensor_mul(
                                apT[:, i_ * T:(i_ + 1) * T],
                                psUu[:], sgT[:])))(),
                            waits=[ulast, w_silu], bump=c_ve)
                        w_apm[i] = w_ap_prev
                    w_apT_ready = w_apm[NPAIR - 1]
                    last_e = (e == n_experts - 1)
                    w_ds = [None] * 4
                    for oc in range(4):
                        s = oc % 4
                        ch = dch["dnb%d" % s]
                        P.emit("gpsimd",
                               (lambda s_=s, oc_=oc, d3=dn3:
                                (lambda g: g.dma_start(
                                    out=dnb[s_][:].rearrange(
                                        "p (ic j) -> p ic j", ic=NPAIR),
                                    in_=d3[:, :, oc_ * 512:
                                           (oc_ + 1) * 512])))(),
                               waits=[w_dnb_use[s]] + e_bar,
                               bump=ch, amt=16)
                        w_ds[oc] = (ch, ch.count)
                    loop = ([(oc, t_) for oc in range(4) for t_ in range(NT)]
                            if not last_e else
                            [(oc, t_) for t_ in range(NT) for oc in range(4)])
                    for oc, t_ in loop:
                        s = oc % 4
                        b = (t_ + oc) % 2 if last_e else t_ % 2
                        dlast = None
                        for ic in range(NPAIR):
                            dlast = P.emit(
                                "tensor",
                                (lambda s_=s, b_=b, ic_=ic, t__=t_:
                                 (lambda t: t.matmul(
                                     psDn[b_][:],
                                     apT[:, ic_ * T + t__ * 128:
                                         ic_ * T + t__ * 128 + 128],
                                     dnb[s_][:, ic_ * 512:
                                             (ic_ + 1) * 512],
                                     start=(ic_ == 0),
                                     stop=(ic_ == NPAIR - 1))))(),
                                waits=[w_ds[oc], w_apT_ready, w_dn_evac[b]],
                                bump=(c_pe if ic == NPAIR - 1 else None))
                        w_dnb_use[s] = dlast
                        w_dn_evac[b] = P.emit(
                            "vector",
                            (lambda b_=b, t__=t_, oc_=oc, cwc=cwcol,
                                    le=last_e:
                             (lambda v: v.scalar_tensor_tensor(
                                 acc[:, t__ * D + oc_ * 512:
                                     t__ * D + oc_ * 512 + 512],
                                 psDn[b_][:],
                                 cw[:, t__ * NE + cwc:
                                    t__ * NE + cwc + 1],
                                 acc[:, t__ * D + oc_ * 512:
                                     t__ * D + oc_ * 512 + 512],
                                 op0=ALU.mult, op1=ALU.add,
                                 accum_out=(gscA[:, t__ * 64 + 60 + oc_:
                                                 t__ * 64 + 61 + oc_]
                                            if le else None))))(),
                            waits=[dlast, w_cw_ready, w_x1_ready],
                            bump=c_ve)
                        if last_e and oc == 3:
                            w_acc_final[t_] = w_dn_evac[b]

                w_y_tiles = ln_stats(w_acc_final, None, 60)
                for t_ in range(NT):
                    P.emit("sync",
                           (lambda t__=t_: (lambda sy: sy.dma_start(
                               out=y[t__ * 128:(t__ + 1) * 128, :],
                               in_=acc[:, t__ * D:(t__ + 1) * D])))(),
                           waits=[w_y_tiles[t_]], bump=c_so, amt=16)
                P.emit("sync", None, waits=[(c_so, c_so.count)])

        for name, ch in P.chans.items():
            if ch.count > 0:
                ch.sem = es.enter_context(nc.semaphore(name))
        blk = es.enter_context(nc.Block())

        def replay(engname):
            def body(eng):
                for ws, fn, bump, amt in P.ops[engname]:
                    for chn, tok in ws:
                        if chn.sem is not None:
                            eng.wait_ge(chn.sem, tok)
                    if fn is None:
                        continue
                    ins = fn(eng)
                    if bump is not None and bump.sem is not None:
                        ins.then_inc(bump.sem, amt)
            return body

        blk.tensor(replay("tensor"))
        blk.vector(replay("vector"))
        blk.scalar(replay("scalar"))
        blk.gpsimd(replay("gpsimd"))
        blk.sync(replay("sync"))
    return nc



def _prep(inputs):
    f32 = lambda a: np.ascontiguousarray(np.asarray(a, dtype=np.float32))
    x = f32(inputs["L0_attn_x"])
    wp = f32(inputs["L0_attn_w_project"])
    wo = f32(inputs["L0_attn_w_out"])
    gw = f32(inputs["L0_ffn_gate_w"])
    ue = f32(inputs["L0_ffn_expert_up_w"])
    de = f32(inputs["L0_ffn_expert_down_w"])
    su = f32(inputs["L0_ffn_shared_up_w"])
    sd = f32(inputs["L0_ffn_shared_down_w"])

    com = {
        "Wq": np.ascontiguousarray(
            wp[0:2048].reshape(NH, 128, D).transpose(0, 2, 1)),
        "Wk": np.ascontiguousarray(
            wp[2048:4096].reshape(NH, 128, D).transpose(0, 2, 1)),
        "Wv4": np.ascontiguousarray(
            wp[4096:6144].reshape(4, 512, D).transpose(0, 2, 1)),
        "Wo": np.ascontiguousarray(wo.T),
        "Gw": np.ascontiguousarray(gw.T),
        "Ue": np.ascontiguousarray(ue.transpose(0, 2, 1)),
        "Dn": np.ascontiguousarray(de.transpose(0, 2, 1)),
        "Su": np.ascontiguousarray(su.T),
        "Sd": np.ascontiguousarray(sd.T),
    }

    masks = []
    for w in range(2):
        M = np.zeros((2, 128, 8 * 512), dtype=np.float32)
        for qc in range(2):
            for ktm in range(8):
                kt = ktm + (8 if qc == 1 else 0)
                gk = kt * 128 + np.arange(128)
                for sq in range(4):
                    slot = qc * 4 + sq
                    gq = PERMS[w][slot] * 128 + np.arange(128)
                    M[qc, :, ktm * 512 + sq * 128:
                      ktm * 512 + sq * 128 + 128] = (
                        gk[:, None] <= gq[None, :]).astype(np.float32)
        masks.append(M)

    qrows = []
    for w in range(2):
        qrows.append(np.concatenate(
            [np.arange(p * 128, (p + 1) * 128) for p in PERMS[w]]))

    in_maps = []
    for c in range(8):
        b, w = c // 2, c % 2
        m = dict(com)
        m["xq"] = np.ascontiguousarray(x[b][qrows[w]])
        m["xk"] = np.ascontiguousarray(x[b])
        m["Mq"] = masks[w]
        in_maps.append(m)
    return in_maps, qrows


def kernel(**inputs):
    in_maps, qrows = _prep(inputs)
    nc = build_program()
    res = run_bass_kernel_spmd(nc, in_maps, core_ids=list(range(8)))
    out = np.empty((4, 2048, 2048), dtype=np.float32)
    for c in range(8):
        b, w = c // 2, c % 2
        out[b][qrows[w]] = res.results[c]["y"]
    return out


# version 07
# speedup vs baseline: 1.0093x; 1.0093x over previous
import numpy as np
from contextlib import ExitStack

import concourse.bass as bass
import concourse.mybir as mybir
from concourse.bass_utils import run_bass_kernel_spmd

F32 = mybir.dt.float32
BF16 = mybir.dt.bfloat16
ALU = mybir.AluOpType
AF = mybir.ActivationFunctionType
AX = mybir.AxisListType

D = 2048
NH = 16
T = 1024
NT = 8
KT = 16
NE = 18
NPAIR = 11
SCALE_QK = float(1.0 / np.sqrt(128.0))
EXP_BIAS = -10.0

PERMS = [[0, 2, 4, 6, 9, 11, 13, 15], [1, 3, 5, 7, 8, 10, 12, 14]]
QC_KTS = [8, 16]

ENGINES = ("tensor", "vector", "scalar", "gpsimd", "sync")


class Chan:
    __slots__ = ("name", "count", "sem")

    def __init__(self, name):
        self.name = name
        self.count = 0
        self.sem = None


class Prog:
    def __init__(self):
        self.ops = {e: [] for e in ENGINES}
        self.hwm = {e: {} for e in ENGINES}
        self.chans = {}

    def chan(self, name):
        if name not in self.chans:
            self.chans[name] = Chan(name)
        return self.chans[name]

    def emit(self, eng, fn, waits=(), bump=None, amt=1):
        ws = []
        for wt in waits:
            if wt is None:
                continue
            ch, tok = wt
            if ch is None or tok is None or tok <= 0:
                continue
            if self.hwm[eng].get(ch.name, 0) >= tok:
                continue
            self.hwm[eng][ch.name] = tok
            ws.append((ch, tok))
        tup = None
        if bump is not None:
            bump.count += amt
            tup = (bump, bump.count)
        self.ops[eng].append((ws, fn, bump, amt))
        return tup


def build_program(dbg=(), n_experts=NE, d_oc=4, d_t=NT):
    nc = bass.Bass("TRN2", target_bir_lowering=False)
    P = Prog()

    xq = nc.dram_tensor("xq", [T, D], F32, kind="ExternalInput").ap()
    xk = nc.dram_tensor("xk", [2048, D], F32, kind="ExternalInput").ap()
    Mq = nc.dram_tensor("Mq", [2, 128, 8 * 512], F32, kind="ExternalInput").ap()
    Wq = nc.dram_tensor("Wq", [NH, D, 128], F32, kind="ExternalInput").ap()
    Wk = nc.dram_tensor("Wk", [NH, D, 128], F32, kind="ExternalInput").ap()
    Wv4 = nc.dram_tensor("Wv4", [4, D, 512], F32, kind="ExternalInput").ap()
    Wo = nc.dram_tensor("Wo", [D, D], F32, kind="ExternalInput").ap()
    Gw = nc.dram_tensor("Gw", [D, 16], F32, kind="ExternalInput").ap()
    Ue = nc.dram_tensor("Ue", [16, D, 2816], F32, kind="ExternalInput").ap()
    Dn = nc.dram_tensor("Dn", [16, 1408, D], F32, kind="ExternalInput").ap()
    Su = nc.dram_tensor("Su", [D, 5632], F32, kind="ExternalInput").ap()
    Sd = nc.dram_tensor("Sd", [2816, D], F32, kind="ExternalInput").ap()
    y = nc.dram_tensor("y", [T, D], F32, kind="ExternalOutput").ap()
    dbg_out = {}
    for name in dbg:
        shp = {"accD": [T, D], "x1": [T, D], "cw": [128, NT * NE],
               "oh": [128, NH * T], "qf": [128, NH * T], "kf": [128, 2048],
               "vt": [128, KT * 512], "acc0": [T, D], "sv": [128, 16],
               "wob_d": [128, 16 * 512], "pd": [128, 512],
               "oh2": [128, NH * T], "ev0": [128, 512]}[name]
        dbg_out[name] = nc.dram_tensor("dbg_" + name, shp, F32,
                                       kind="ExternalOutput").ap()

    def etab(e):
        if e < 16:
            return (Ue[e], 0, 1408, Dn[e], e)
        if e == 16:
            return (Su, 0, 2816, Sd[0:1408, :], 16)
        return (Su, 1408, 4224, Sd[1408:2816, :], 17)

    c_pe = P.chan("pe")
    c_ve = P.chan("ve")
    c_ae = P.chan("ae")
    c_gp = P.chan("gp")
    c_sd = P.chan("sd")
    c_so = P.chan("so")
    dch = {k: P.chan("d_" + k) for k in
           ("xb0", "xb1", "wqb0", "wqb1", "wkb0", "wkb1", "wvb", "mld",
            "wob0", "wob1", "gwb", "upg0", "upg1", "upu0", "upu1",
            "dnb0", "dnb1", "dnb2", "dnb3")}

    def bar():
        return [(c_pe, c_pe.count), (c_ve, c_ve.count), (c_ae, c_ae.count)]

    es = ExitStack()
    with es:
        ident = es.enter_context(nc.sbuf_tensor("ident", [128, 128], BF16))
        ones = es.enter_context(nc.sbuf_tensor("ones", [128, 128], BF16))
        gwb = es.enter_context(nc.sbuf_tensor("gwb", [128, 256], BF16))
        cst = es.enter_context(nc.sbuf_tensor("cst", [128, 8], F32))
        zz4 = es.enter_context(nc.sbuf_tensor("zz4", [128, 4], F32))

        P.emit("gpsimd", lambda g: g.memset(ident[:], 0.0), bump=c_gp)
        w_ident = P.emit(
            "gpsimd",
            lambda g: g.affine_select(
                out=ident[:], in_=ident[:], compare_op=ALU.not_equal,
                fill=1.0, base=0, pattern=[[-1, 128]], channel_multiplier=1),
            waits=[(c_gp, 1)], bump=c_gp)
        w_ones = P.emit("gpsimd", lambda g: g.memset(ones[:], 1.0), bump=c_gp)
        w_zz4 = P.emit("gpsimd", lambda g: g.memset(zz4[:], 0.0), bump=c_gp)
        for i, val in enumerate((EXP_BIAS, 1e-5, -0.5, SCALE_QK)):
            w_cst = P.emit("gpsimd",
                           (lambda i_=i, v_=val:
                            (lambda g: g.memset(cst[:, i_:i_ + 1], v_)))(),
                           bump=c_gp)

        qstack = ExitStack()
        Qf = qstack.enter_context(nc.sbuf_tensor("Qf", [128, NH * T], BF16))
        OHf = qstack.enter_context(nc.sbuf_tensor("OHf", [128, NH * T], BF16))

        att = ExitStack()
        with att:
            xT = att.enter_context(nc.sbuf_tensor("xT", [128, 16 * 2048], BF16))
            Mb = att.enter_context(nc.sbuf_tensor("Mb", [128, 2 * 4096], BF16))

            w_xT_ready = None
            w_xqT_ready = None
            w_Qf_ready = None
            w_Q_lastmm = None
            psc = ExitStack()
            with psc:
                psT = [psc.enter_context(nc.psum_tensor("psT%d" % i,
                                                        [128, 1024], BF16))
                       for i in range(2)]
                xbq = ExitStack()
                with xbq:
                    xb = [xbq.enter_context(nc.sbuf_tensor(
                        "xb%d" % i, [128, D], BF16)) for i in range(2)]
                    w_xb_use = [None, None]
                    w_psT_evac = [None, None]

                    def build_T(src_ap, dst, kt_idx, ncols, parts):
                        s = kt_idx % 2
                        ch = dch["xb%d" % s]
                        P.emit("gpsimd",
                               (lambda a=src_ap, s_=s:
                                (lambda g: g.dma_start(out=xb[s_][:], in_=a)))(),
                               waits=[w_xb_use[s]], bump=ch, amt=16)
                        w_dma = (ch, ch.count)
                        dst3 = dst.rearrange("p (dc k) -> p dc k", k=ncols)
                        for half in range(2):
                            b = half
                            last = None
                            for j in range(8):
                                dc = half * 8 + j
                                last = P.emit(
                                    "tensor",
                                    (lambda b_=b, j_=j, s_=s, dc_=dc:
                                     (lambda t: t.transpose(
                                         psT[b_][:, j_ * 128:(j_ + 1) * 128],
                                         xb[s_][:, dc_ * 128:(dc_ + 1) * 128],
                                         ident[:])))(),
                                    waits=[w_dma, w_ident, w_psT_evac[b]],
                                    bump=(c_pe if j == 7 else None))
                            w_xb_use[s] = last
                            ov = dst3[:, half * 8:half * 8 + 8,
                                      kt_idx * 128:kt_idx * 128 + 128]
                            iv = psT[b][:, 0:1024].rearrange(
                                "p (dc k) -> p dc k", k=128)
                            w_psT_evac[b] = P.emit(
                                "vector",
                                (lambda o=ov, i=iv:
                                 (lambda v: v.tensor_copy(o, i)))(),
                                waits=[last], bump=c_ve)
                            parts.append(w_psT_evac[b])

                    pts = []
                    for kt in range(KT):
                        build_T(xk[kt * 128:(kt + 1) * 128, :], xT[:], kt,
                                2048, pts)
                    w_xT_ready = pts[-1]
                    for qc in range(2):
                        P.emit("gpsimd",
                               (lambda qc_=qc: (lambda g: g.dma_start(
                                   out=Mb[:, qc_ * 4096:(qc_ + 1) * 4096],
                                   in_=Mq[qc_])))(),
                               bump=dch["mld"], amt=16)
                    w_mask_ld = (dch["mld"], dch["mld"].count)
                    pts2 = []
                    xqT = xbq.enter_context(
                        nc.sbuf_tensor("xqT", [128, 16 * T], BF16))
                    for qt in range(NT):
                        build_T(xq[qt * 128:(qt + 1) * 128, :], xqT[:], qt,
                                T, pts2)
                    w_xqT_ready = pts2[-1]

                    wqb = [xbq.enter_context(nc.sbuf_tensor(
                        "wqb%d" % i, [128, 16 * 128], BF16)) for i in range(2)]
                    psq = ExitStack()
                    with psq:
                        psQ = [psq.enter_context(nc.psum_tensor(
                            "psQ%d" % i, [128, 512], F32)) for i in range(4)]
                        w_wqb_use = [None, None]
                        w_psQ_evac = [None, None, None, None]
                        for h in range(NH):
                            s = h % 2
                            ch = dch["wqb%d" % s]
                            P.emit("gpsimd",
                                   (lambda s_=s, h_=h: (lambda g: g.dma_start(
                                       out=wqb[s_][:].rearrange(
                                           "p (c j) -> p c j", c=16),
                                       in_=Wq[h_].rearrange(
                                           "(c p) j -> p c j", p=128))))(),
                                   waits=[w_wqb_use[s]], bump=ch, amt=16)
                            w_dma = (ch, ch.count)
                            lastmm = None
                            hb = (h % 2) * 2
                            for d in range(16):
                                for q2 in range(2):
                                    lastmm = P.emit(
                                        "tensor",
                                        (lambda s_=s, d_=d, q2_=q2, hb_=hb:
                                         (lambda t: t.matmul(
                                             psQ[hb_ + q2_][:],
                                             wqb[s_][:, d_ * 128:
                                                     (d_ + 1) * 128],
                                             xqT[:, d_ * T + q2_ * 512:
                                                 d_ * T + q2_ * 512 + 512],
                                             start=(d_ == 0),
                                             stop=(d_ == 15))))(),
                                        waits=[w_dma, w_xqT_ready,
                                               w_psQ_evac[hb + q2]],
                                        bump=(c_pe if d == 15 else None))
                            w_wqb_use[s] = lastmm
                            w_Q_lastmm = lastmm
                            for q2 in range(2):
                                w_psQ_evac[hb + q2] = P.emit(
                                    "scalar",
                                    (lambda h_=h, q2_=q2, hb_=hb:
                                     (lambda sc: sc.mul(
                                        Qf[:, h_ * T + q2_ * 512:
                                           h_ * T + q2_ * 512 + 512],
                                        psQ[hb_ + q2_][:], cst[:, 3:4])))(),
                                    waits=[lastmm, w_cst], bump=c_ae)
                        w_Qf_ready = w_psQ_evac[hb + 1]

            Kf2 = [att.enter_context(nc.sbuf_tensor("Kf%d" % i,
                                                     [128, 2048], BF16))
                   for i in range(2)]
            Vtok = att.enter_context(nc.sbuf_tensor("Vtok", [128, KT * 512],
                                                    BF16))
            PT = att.enter_context(nc.sbuf_tensor("PT", [128, 8 * 512], BF16))
            rZb = att.enter_context(nc.sbuf_tensor("rZb", [128, 512], F32))
            wkb = [att.enter_context(nc.sbuf_tensor("wkb%d" % i,
                                                    [128, 16 * 128], BF16))
                   for i in range(2)]
            wvb = att.enter_context(nc.sbuf_tensor("wvb", [128, 16 * 512],
                                                   BF16))

            bc_bar = [w_Q_lastmm, w_Qf_ready, w_xqT_ready]

            psa = ExitStack()
            with psa:
                psP = [psa.enter_context(nc.psum_tensor("psP%d" % i,
                                                        [128, 512], F32))
                       for i in range(2)]
                psS = psa.enter_context(nc.psum_tensor("psS", [128, 2048],
                                                       F32))
                psZ = psa.enter_context(nc.psum_tensor("psZ", [128, 512], F32))
                psO = psa.enter_context(nc.psum_tensor("psO", [128, 512], F32))

                w_wkb_use = [w_Q_lastmm, w_Q_lastmm]
                w_wvb_use = w_Q_lastmm
                w_mask_last = [None]
                w_xT_last = [None]
                w_psP_evac = [w_Qf_ready, w_Qf_ready]
                w_Vtok_ready = None
                w_recip_prev = None
                w_oh_prev = None
                w_oh_all = []
                w_Kf_ready = [None, None]
                wkb_dma = [None] * (NH + 2)

                def emit_wkb_load(h_):
                    s_ = h_ % 2
                    chk = dch["wkb%d" % s_]
                    P.emit("gpsimd",
                           (lambda s__=s_, h__=h_: (lambda g: g.dma_start(
                               out=wkb[s__][:].rearrange("p (c j) -> p c j",
                                                         c=16),
                               in_=Wk[h__].rearrange("(c p) j -> p c j",
                                                     p=128))))(),
                           waits=[w_wkb_use[s_]] + bc_bar, bump=chk, amt=16)
                    wkb_dma[h_] = (chk, chk.count)

                def kproj_chunk(h_, kc):
                    s_ = h_ % 2
                    b = kc % 2
                    lastmm = None
                    for d in range(16):
                        lastmm = P.emit(
                            "tensor",
                            (lambda s__=s_, b_=b, d_=d, kc_=kc:
                             (lambda t: t.matmul(
                                 psP[b_][:],
                                 wkb[s__][:, d_ * 128:(d_ + 1) * 128],
                                 xT[:, d_ * 2048 + kc_ * 512:
                                    d_ * 2048 + kc_ * 512 + 512],
                                 start=(d_ == 0), stop=(d_ == 15))))(),
                            waits=[wkb_dma[h_], w_psP_evac[b], w_xT_ready],
                            bump=(c_pe if d == 15 else None))
                    w_psP_evac[b] = P.emit(
                        "scalar",
                        (lambda s__=s_, kc_=kc, b_=b: (lambda sc: sc.copy(
                            Kf2[s__][:, kc_ * 512:(kc_ + 1) * 512],
                            psP[b_][:])))(),
                        waits=[lastmm], bump=c_ae)
                    w_wkb_use[s_] = lastmm
                    w_xT_last[0] = lastmm
                    if kc == 3:
                        w_Kf_ready[s_] = w_psP_evac[b]

                emit_wkb_load(0)
                emit_wkb_load(1)
                for kc in range(4):
                    kproj_chunk(0, kc)
                kproj_chunk(1, 0)
                kproj_chunk(1, 1)

                def emit_wvb_load(grp_):
                    P.emit("gpsimd",
                           (lambda g_=grp_: (lambda g: g.dma_start(
                               out=wvb[:].rearrange("p (c j) -> p c j",
                                                    c=16),
                               in_=Wv4[g_].rearrange(
                                   "(c p) j -> p c j", p=128))))(),
                           waits=[w_wvb_use] + bc_bar,
                           bump=dch["wvb"], amt=16)
                    wvb_dma[0] = (dch["wvb"], dch["wvb"].count)

                wvb_dma = [None]
                emit_wvb_load(0)
                for h in range(NH):
                    if h + 2 < NH:
                        emit_wkb_load(h + 2)
                    if h % 4 == 2 and (h + 2) // 4 < 4:
                        emit_wvb_load((h + 2) // 4)
                    if h % 4 == 0:
                        w_dma = wvb_dma[0]
                        vts = []
                        lastmm = None
                        for kt in range(KT):
                            b = kt % 2
                            for d in range(16):
                                lastmm = P.emit(
                                    "tensor",
                                    (lambda b_=b, d_=d, kt_=kt:
                                     (lambda t: t.matmul(
                                         psP[b_][:],
                                         xT[:, d_ * 2048 + kt_ * 128:
                                            d_ * 2048 + kt_ * 128 + 128],
                                         wvb[:, d_ * 512:(d_ + 1) * 512],
                                         start=(d_ == 0), stop=(d_ == 15))))(),
                                    waits=[w_dma, w_psP_evac[b], w_xT_ready],
                                    bump=(c_pe if d == 15 else None))
                            w_psP_evac[b] = P.emit(
                                "vector",
                                (lambda kt_=kt, b_=b: (lambda v: v.tensor_copy(
                                    Vtok[:, kt_ * 512:(kt_ + 1) * 512],
                                    psP[b_][:])))(),
                                waits=[lastmm], bump=c_ve)
                            vts.append(w_psP_evac[b])
                        w_wvb_use = lastmm
                        w_xT_last[0] = lastmm
                        w_Vtok_ready = vts[-1]

                    if h + 1 >= NH:
                        pend = []
                    elif h == 0:
                        pend = [2, 3]
                    else:
                        pend = list(range(4))

                    def inject(n):
                        for _ in range(n):
                            if pend:
                                kproj_chunk(h + 1, pend.pop(0))

                    for qc in range(2):
                        kts = QC_KTS[qc]
                        qoff = qc * 512
                        w_Z_stop = None
                        w_O_stop = None
                        for ktg in range(0, kts, 8):
                            exp_toks = []
                            for kt2 in range(ktg, ktg + 8, 2):
                                half = (kt2 // 2) % 2
                                st_last = None
                                for kt in (kt2, kt2 + 1):
                                    col = half * 1024 + (kt % 2) * 512
                                    st_last = P.emit(
                                        "tensor",
                                        (lambda kt_=kt, h_=h, qoff_=qoff,
                                                col_=col:
                                         (lambda t: t.matmul(
                                             psS[:, col_:col_ + 512],
                                             Kf2[h_ % 2][:, kt_ * 128:
                                                         (kt_ + 1) * 128],
                                             Qf[:, h_ * T + qoff_:
                                                h_ * T + qoff_ + 512],
                                             start=True, stop=True)))(),
                                        waits=[w_Kf_ready[h % 2], w_Qf_ready,
                                               exp_toks[-2] if
                                               len(exp_toks) >= 2 else None,
                                               w_recip_prev, w_oh_prev],
                                        bump=(c_pe if kt == kt2 + 1
                                              else None))
                                tok_e = P.emit(
                                    "scalar",
                                    (lambda kt2_=kt2, ktg_=ktg, half_=half:
                                     (lambda sc: sc.activation(
                                         PT[:, (kt2_ - ktg_) * 512:
                                            (kt2_ - ktg_) * 512 + 1024],
                                         psS[:, half_ * 1024:
                                             half_ * 1024 + 1024],
                                         AF.Exp, bias=cst[:, 0:1],
                                         scale=1.0)))(),
                                    waits=[st_last, w_cst], bump=c_ae)
                                exp_toks.append(tok_e)
                            inject(2 if (qc == 0) else 1)
                            masked = (qc == 0 and ktg == 0) or \
                                     (qc == 1 and ktg == 8)
                            if masked:
                                mq = 0 if qc == 0 else 1
                                w_mask = P.emit(
                                    "vector",
                                    (lambda mq_=mq: (lambda v: v.tensor_mul(
                                        PT[:, 0:4096], PT[:, 0:4096],
                                        Mb[:, mq_ * 4096:
                                           (mq_ + 1) * 4096])))(),
                                    waits=[exp_toks[-1], w_mask_ld],
                                    bump=c_ve)
                                w_mask_last[0] = w_mask
                                pt_wait = [w_mask]
                            else:
                                pt_wait = [exp_toks[-1]]
                            for kt in range(ktg, ktg + 8):
                                w_Z_stop = P.emit(
                                    "tensor",
                                    (lambda kt_=kt, ktg_=ktg, kts_=kts:
                                     (lambda t: t.matmul(
                                         psZ[:], ones[:],
                                         PT[:, (kt_ - ktg_) * 512:
                                            (kt_ - ktg_) * 512 + 512],
                                         start=(kt_ == 0),
                                         stop=(kt_ == kts_ - 1))))(),
                                    waits=pt_wait + [w_ones, w_recip_prev],
                                    bump=(c_pe if kt == kts - 1 else None))
                            for kt in range(ktg, ktg + 8):
                                w_O_stop = P.emit(
                                    "tensor",
                                    (lambda kt_=kt, ktg_=ktg, kts_=kts, h_=h:
                                     (lambda t: t.matmul(
                                         psO[:],
                                         Vtok[:, kt_ * 512 + (h_ % 4) * 128:
                                              kt_ * 512 + (h_ % 4) * 128
                                              + 128],
                                         PT[:, (kt_ - ktg_) * 512:
                                            (kt_ - ktg_) * 512 + 512],
                                         start=(kt_ == 0),
                                         stop=(kt_ == kts_ - 1))))(),
                                    waits=pt_wait + [w_Vtok_ready, w_oh_prev],
                                    bump=(c_pe if kt == kts - 1 else None))
                        w_recip_prev = P.emit(
                            "vector",
                            lambda v: v.reciprocal(rZb[:], psZ[:]),
                            waits=[w_Z_stop], bump=c_ve)
                        w_oh_prev = P.emit(
                            "vector",
                            (lambda h_=h, qoff_=qoff: (lambda v: v.tensor_mul(
                                OHf[:, h_ * T + qoff_:h_ * T + qoff_ + 512],
                                psO[:], rZb[:])))(),
                            waits=[w_O_stop, w_recip_prev], bump=c_ve)
                        w_oh_all.append(w_oh_prev)
                    inject(4)
                attn_pe_bar = (c_pe, c_pe.count)
                d_psP_guard = [w_psP_evac[0], w_psP_evac[1]]
                w_OHf_ready = w_oh_all[-1]

            if "qf" in dbg_out:
                P.emit("gpsimd", lambda g: g.dma_start(
                    out=dbg_out["qf"], in_=Qf[:]),
                    waits=[w_Qf_ready], bump=c_so, amt=16)
            if "kf" in dbg_out:
                P.emit("gpsimd", lambda g: g.dma_start(
                    out=dbg_out["kf"], in_=Kf[:]),
                    waits=[(c_ae, c_ae.count)], bump=c_so, amt=16)
            if "vt" in dbg_out:
                P.emit("gpsimd", lambda g: g.dma_start(
                    out=dbg_out["vt"], in_=Vtok[:]),
                    waits=[(c_ve, c_ve.count)], bump=c_so, amt=16)
            if "oh" in dbg_out:
                P.emit("gpsimd", lambda g: g.dma_start(
                    out=dbg_out["oh"], in_=OHf[:]),
                    waits=[w_OHf_ready], bump=c_so, amt=16)

        acc = es.enter_context(nc.sbuf_tensor("acc", [128, NT * D], F32,
                                              side="right"))
        x1T = es.enter_context(nc.sbuf_tensor("x1T", [128, 16 * T], BF16,
                                              side="right"))
        xb2 = [es.enter_context(nc.sbuf_tensor("xb2_%d" % i, [128, D], BF16,
                                               side="right"))
               for i in range(2)]
        cw = es.enter_context(nc.sbuf_tensor("cw", [128, NT * NE], F32,
                                             side="right"))
        sva = es.enter_context(nc.sbuf_tensor("sva", [128, NT * 8], F32,
                                              side="right"))
        gscA = es.enter_context(nc.sbuf_tensor("gscA", [128, NT * 64], F32,
                                               side="right"))
        gscB = es.enter_context(nc.sbuf_tensor("gscB", [128, NT * 64], F32,
                                               side="right"))

        def A_(t_, a, b=None):
            if b is None:
                b = a + 1
            return gscA[:, t_ * 64 + a:t_ * 64 + b]

        def B_(t_, a, b=None):
            if b is None:
                b = a + 1
            return gscB[:, t_ * 64 + a:t_ * 64 + b]

        def SV(t_, j):
            return sva[:, t_ * 8 + j:t_ * 8 + j + 1]

        w_gwb = P.emit(
            "gpsimd",
            lambda g: g.dma_start(
                out=gwb[:].rearrange("p (c j) -> p c j", c=16),
                in_=Gw.rearrange("(c p) j -> p c j", p=128)),
            bump=dch["gwb"], amt=16)

        d_bar = bar()
        w_ae_bar = (c_ae, c_ae.count)
        for t_ in range(NT):
            P.emit("sync",
                   (lambda t__=t_: (lambda sy: sy.dma_start(
                       out=acc[:, t__ * D:(t__ + 1) * D],
                       in_=xq[t__ * 128:(t__ + 1) * 128, :])))(),
                   waits=d_bar, bump=c_sd, amt=16)
        w_acc_ld = (c_sd, c_sd.count)

        w_acc_done = [None] * NT
        w_xb2_rd = [None, None]
        w_xb2_wr = [None, None]
        w_psTD_evac = [None, None]

        def ln_stats(w_ready, w_pre, pbase):
            s1t, s2t, t1, t2, t3 = [], [], [], [], []
            for t_ in range(NT):
                s1t.append(P.emit(
                    "vector",
                    (lambda t__=t_: (lambda v: v.reduce_sum(
                        SV(t__, 0),
                        gscA[:, t__ * 64 + pbase:t__ * 64 + pbase + 4],
                        axis=AX.X)))(),
                    waits=[w_ready[t_], w_pre], bump=c_ve))
            for t_ in range(NT):
                s2t.append(P.emit(
                    "scalar",
                    (lambda t__=t_: (lambda sc: sc.activation(
                        xb2[t__ % 2][:], acc[:, t__ * D:(t__ + 1) * D],
                        AF.Square, accum_out=SV(t__, 1))))(),
                    waits=[w_ready[t_], w_pre, w_xb2_rd[t_ % 2],
                           w_xb2_wr[t_ % 2]],
                    bump=c_ae))
                w_xb2_wr[t_ % 2] = s2t[-1]
            for t_ in range(NT):
                t1.append(P.emit(
                    "vector",
                    (lambda t__=t_: (lambda v: v.tensor_scalar(
                        SV(t__, 2), SV(t__, 0), -1.0 / D, None,
                        op0=ALU.mult)))(),
                    waits=[s1t[t_]], bump=c_ve))
            for t_ in range(NT):
                t1[t_] = P.emit(
                    "vector",
                    (lambda t__=t_: (lambda v: v.tensor_mul(
                        SV(t__, 3), SV(t__, 2), SV(t__, 2))))(),
                    waits=[t1[t_]], bump=c_ve)
            for t_ in range(NT):
                t2.append(P.emit(
                    "vector",
                    (lambda t__=t_: (lambda v: v.scalar_tensor_tensor(
                        SV(t__, 4), SV(t__, 1), 1.0 / D, SV(t__, 3),
                        op0=ALU.mult, op1=ALU.subtract)))(),
                    waits=[t1[t_], s2t[t_]], bump=c_ve))
            for t_ in range(NT):
                t2[t_] = P.emit(
                    "scalar",
                    (lambda t__=t_: (lambda sc: sc.activation(
                        SV(t__, 5), SV(t__, 4), AF.Ln,
                        bias=cst[:, 1:2], scale=1.0)))(),
                    waits=[t2[t_], w_cst], bump=c_ae)
            for t_ in range(NT):
                t2[t_] = P.emit(
                    "scalar",
                    (lambda t__=t_: (lambda sc: sc.activation(
                        SV(t__, 5), SV(t__, 5), AF.Exp,
                        bias=0.0, scale=cst[:, 2:3])))(),
                    waits=[t2[t_]], bump=c_ae)
            for t_ in range(NT):
                t3.append(P.emit(
                    "vector",
                    (lambda t__=t_: (lambda v: v.tensor_scalar(
                        acc[:, t__ * D:(t__ + 1) * D],
                        acc[:, t__ * D:(t__ + 1) * D],
                        SV(t__, 2), SV(t__, 5),
                        op0=ALU.add, op1=ALU.mult)))(),
                    waits=[t2[t_]], bump=c_ve))
            return t3

        dsc = ExitStack()
        with dsc:
            wob = [dsc.enter_context(nc.sbuf_tensor("wob%d" % i,
                                                    [128, 16 * 512], BF16))
                   for i in range(2)]
            psd = ExitStack()
            with psd:
                psD2 = [psd.enter_context(nc.psum_tensor("psD2%d" % i,
                                                         [128, 512], F32))
                        for i in range(2)]
                psTD = [psd.enter_context(nc.psum_tensor("psTD%d" % i,
                                                         [128, 1024], BF16))
                        for i in range(2)]
                psG = [psd.enter_context(nc.psum_tensor("psG%d" % i,
                                                        [128, 16], F32))
                       for i in range(2)]

                w_wob_use = [[w_xT_last[0]], [w_xT_last[0]]]
                w_outproj_last = [None]
                w_psD2_evac = [None, None]
                first_mm = True
                for oc in range(d_oc):
                    s = oc % 2
                    ch = dch["wob%d" % s]
                    P.emit("gpsimd",
                           (lambda s_=s, oc_=oc: (lambda g: g.dma_start(
                               out=wob[s_][:].rearrange("p (c j) -> p c j",
                                                        c=16),
                               in_=Wo.rearrange("(c p) o -> p c o",
                                                p=128)[:, :, oc_ * 512:
                                                       (oc_ + 1) * 512])))(),
                           waits=list(w_wob_use[s]), bump=ch, amt=16)
                    w_dma = (ch, ch.count)
                    for t_ in range(NT):
                        b = t_ % 2
                        lastmm = None
                        for hc in range(16):
                            lastmm = P.emit(
                                "tensor",
                                (lambda s_=s, b_=b, hc_=hc, t__=t_:
                                 (lambda t: t.matmul(
                                     psD2[b_][:],
                                     OHf[:, hc_ * T + t__ * 128:
                                         hc_ * T + t__ * 128 + 128],
                                     wob[s_][:, hc_ * 512:(hc_ + 1) * 512],
                                     start=(hc_ == 0), stop=(hc_ == 15))))(),
                                waits=[w_dma, w_OHf_ready, w_psD2_evac[b]]
                                + (d_psP_guard if first_mm else []),
                                bump=(c_pe if hc == 15 else None))
                            first_mm = False
                        w_wob_use[s] = [lastmm]
                        w_outproj_last[0] = lastmm
                        w_psD2_evac[b] = P.emit(
                            "vector",
                            (lambda t__=t_, oc_=oc, b_=b:
                             (lambda v: v.scalar_tensor_tensor(
                                 acc[:, t__ * D + oc_ * 512:
                                     t__ * D + oc_ * 512 + 512],
                                 psD2[b_][:], 1.0,
                                 acc[:, t__ * D + oc_ * 512:
                                     t__ * D + oc_ * 512 + 512],
                                 op0=ALU.mult, op1=ALU.add,
                                 accum_out=gscA[:, t__ * 64 + 56 + oc_:
                                                t__ * 64 + 57 + oc_])))(),
                            waits=[lastmm, w_acc_ld], bump=c_ve)
                        if oc == d_oc - 1:
                            w_acc_done[t_] = w_psD2_evac[b]

                w_dump = None
                if "accD" in dbg_out:
                    for t_ in range(NT):
                        w_dump = P.emit(
                            "sync",
                            (lambda t__=t_: (lambda sy: sy.dma_start(
                                out=dbg_out["accD"][t__ * 128:
                                                    (t__ + 1) * 128, :],
                                in_=acc[:, t__ * D:(t__ + 1) * D])))(),
                            waits=[w_acc_done[t_]], bump=c_so, amt=16)

                w_x1_tiles = ln_stats(w_acc_done, w_dump, 56)
                w_x1_ready = w_x1_tiles[-1]

                x1T3 = x1T[:].rearrange("p (dc k) -> p dc k", k=T)
                w_psG_rd = [[None, None], [None, None]]
                w_x1T_evac = [None] * NT
                w_cw_tiles = [None] * NT
                w_gsA_init = [None] * NT
                for t_ in range(NT):
                    w_gsA_init[t_] = P.emit(
                        "vector",
                        (lambda t__=t_: (lambda v: v.memset(
                            A_(t__, 20, 24), -1e9)))(),
                        waits=d_bar, bump=c_ve)

                def gate_stage(pair, fn_t, eng="vector", waits_t=None,
                               bump=c_ve):
                    out = {}
                    for t_ in pair:
                        out[t_] = P.emit(eng, fn_t(t_),
                                         waits=(waits_t(t_) if waits_t
                                                else []),
                                         bump=bump)
                    return out

                for p0 in range(0, NT, 2):
                    pair = (p0, p0 + 1)
                    for t_ in pair:
                        w_cast = P.emit(
                            "scalar",
                            (lambda t__=t_: (lambda sc: sc.copy(
                                xb2[t__ % 2][:],
                                acc[:, t__ * D:(t__ + 1) * D])))(),
                            waits=[w_x1_tiles[t_], w_xb2_rd[t_ % 2],
                                   w_xb2_wr[t_ % 2]],
                            bump=c_ae)
                        w_xb2_wr[t_ % 2] = w_cast
                        for half in range(2):
                            b = half
                            last = None
                            for j in range(8):
                                dc = half * 8 + j
                                last = P.emit(
                                    "tensor",
                                    (lambda b_=b, j_=j, dc_=dc, t__=t_:
                                     (lambda t: t.transpose(
                                         psTD[b_][:, j_ * 128:(j_ + 1) * 128],
                                         xb2[t__ % 2][:, dc_ * 128:
                                                      (dc_ + 1) * 128],
                                         ident[:])))(),
                                    waits=[w_cast, w_psTD_evac[b], w_ident],
                                    bump=(c_pe if j == 7 else None))
                            w_xb2_rd[t_ % 2] = last
                            ov = x1T3[:, half * 8:half * 8 + 8,
                                      t_ * 128:t_ * 128 + 128]
                            iv = psTD[b][:, 0:1024].rearrange(
                                "p (dc k) -> p dc k", k=128)
                            w_psTD_evac[b] = P.emit(
                                "vector",
                                (lambda o=ov, i=iv:
                                 (lambda v: v.tensor_copy(o, i)))(),
                                waits=[last], bump=c_ve)
                        w_x1T_evac[t_] = w_psTD_evac[1]
                    w_gmm = {}
                    for t_ in pair:
                        lastg = None
                        for d in range(16):
                            lastg = P.emit(
                                "tensor",
                                (lambda d_=d, t__=t_: (lambda t: t.matmul(
                                    psG[t__ % 2][:],
                                    x1T[:, d_ * T + t__ * 128:
                                        d_ * T + t__ * 128 + 128],
                                    gwb[:, d_ * 16:(d_ + 1) * 16],
                                    start=(d_ == 0), stop=(d_ == 15))))(),
                                waits=[w_gwb, w_x1T_evac[t_],
                                       w_psG_rd[t_ % 2][0],
                                       w_psG_rd[t_ % 2][1]],
                                bump=(c_pe if d == 15 else None))
                        w_gmm[t_] = lastg
                    tk = gate_stage(pair, lambda t_: (
                        lambda t__=t_: (lambda v: v.reduce_max(
                            A_(t__, 52), psG[t__ % 2][:], axis=AX.X,
                            negate=True)))(),
                        waits_t=lambda t_: [w_gmm[t_]])
                    for t_ in pair:
                        w_psG_rd[t_ % 2][0] = tk[t_]
                    tke = {}
                    for t_ in pair:
                        tke[t_] = P.emit(
                            "scalar",
                            (lambda t__=t_: (lambda sc: sc.activation(
                                A_(t__, 0, 16), psG[t__ % 2][:], AF.Exp,
                                bias=A_(t__, 52), scale=1.0,
                                accum_out=A_(t__, 53))))(),
                            waits=[tk[t_]], bump=c_ae)
                        w_psG_rd[t_ % 2][1] = tke[t_]
                    tk = gate_stage(pair, lambda t_: (
                        lambda t__=t_: (lambda v: v.reciprocal(
                            A_(t__, 54), A_(t__, 53))))(),
                        waits_t=lambda t_: [tke[t_]])
                    tk = gate_stage(pair, lambda t_: (
                        lambda t__=t_: (lambda v: v.tensor_single_scalar(
                            A_(t__, 0, 16), A_(t__, 0, 16), A_(t__, 54),
                            ALU.mult)))(),
                        waits_t=lambda t_: [tk[t_]])
                    for j in range(4):
                        tk = gate_stage(pair, lambda t_: (
                            lambda t__=t_, j_=j: (lambda v: v.reduce_max(
                                A_(t__, 16 + j_), A_(t__, 4 * j_, 4 * j_ + 4),
                                axis=AX.X)))(),
                            waits_t=lambda t_: [tk[t_]])
                    tk = gate_stage(pair, lambda t_: (
                        lambda t__=t_: (lambda v: v.max(
                            A_(t__, 24, 32), A_(t__, 16, 24))))(),
                        waits_t=lambda t_: [tk[t_], w_gsA_init[t_]])
                    tk = gate_stage(pair, lambda t_: (
                        lambda t__=t_: (lambda v: v.tensor_single_scalar(
                            A_(t__, 32, 36), A_(t__, 16, 20), A_(t__, 26),
                            ALU.is_gt)))(),
                        waits_t=lambda t_: [tk[t_]])
                    for j in range(4):
                        tk = gate_stage(pair, lambda t_: (
                            lambda t__=t_, j_=j:
                            (lambda v: v.tensor_single_scalar(
                                A_(t__, 36 + 4 * j_, 40 + 4 * j_), zz4[:],
                                A_(t__, 32 + j_), ALU.add)))(),
                            waits_t=lambda t_: [tk[t_], w_zz4])
                    tk = gate_stage(pair, lambda t_: (
                        lambda t__=t_: (lambda v: v.tensor_mul(
                            B_(t__, 0, 16), A_(t__, 0, 16),
                            A_(t__, 36, 52))))(),
                        waits_t=lambda t_: [tk[t_]])
                    tk = gate_stage(pair, lambda t_: (
                        lambda t__=t_: (lambda v: v.max(
                            B_(t__, 16, 24), B_(t__, 0, 16))))(),
                        waits_t=lambda t_: [tk[t_]])
                    tk = gate_stage(pair, lambda t_: (
                        lambda t__=t_: (lambda v: v.tensor_single_scalar(
                            B_(t__, 24, 40), B_(t__, 0, 16), B_(t__, 20),
                            ALU.is_gt)))(),
                        waits_t=lambda t_: [tk[t_]])
                    tk = gate_stage(pair, lambda t_: (
                        lambda t__=t_: (lambda v: v.tensor_mul(
                            B_(t__, 40, 56), B_(t__, 0, 16),
                            B_(t__, 24, 40))))(),
                        waits_t=lambda t_: [tk[t_]])
                    tk = gate_stage(pair, lambda t_: (
                        lambda t__=t_: (lambda v: v.reduce_sum(
                            B_(t__, 56), B_(t__, 40, 56), axis=AX.X)))(),
                        waits_t=lambda t_: [tk[t_]])
                    tk = gate_stage(pair, lambda t_: (
                        lambda t__=t_: (lambda v: v.tensor_single_scalar(
                            B_(t__, 56), B_(t__, 56), 1e-20, ALU.add)))(),
                        waits_t=lambda t_: [tk[t_]])
                    tk = gate_stage(pair, lambda t_: (
                        lambda t__=t_: (lambda v: v.reciprocal(
                            B_(t__, 56), B_(t__, 56))))(),
                        waits_t=lambda t_: [tk[t_]])
                    tk = gate_stage(pair, lambda t_: (
                        lambda t__=t_: (lambda v: v.tensor_single_scalar(
                            cw[:, t__ * NE:t__ * NE + 16], B_(t__, 40, 56),
                            B_(t__, 56), ALU.mult)))(),
                        waits_t=lambda t_: [tk[t_]])
                    for t_ in pair:
                        w_cw_tiles[t_] = tk[t_]

                w_cw_ready = P.emit(
                    "vector",
                    lambda v: v.memset(
                        cw[:].rearrange("p (t e) -> p t e",
                                        e=NE)[:, :, 16:18], 1.0),
                    waits=[w_cw_tiles[-1]], bump=c_ve)
                w_x1T_ready = w_psTD_evac[1]
                w_gate_last_ve = w_cw_ready
                w_gate_last_ae = tke[pair[-1]]

            if "x1" in dbg_out:
                for t_ in range(NT):
                    P.emit("sync",
                           (lambda t__=t_: (lambda sy: sy.dma_start(
                               out=dbg_out["x1"][t__ * 128:(t__ + 1) * 128, :],
                               in_=acc[:, t__ * D:(t__ + 1) * D])))(),
                           waits=[w_x1_ready], bump=c_so, amt=16)
            if "cw" in dbg_out:
                P.emit("sync", lambda sy: sy.dma_start(
                    out=dbg_out["cw"], in_=cw[:]),
                    waits=[w_cw_ready], bump=c_so, amt=16)
        qstack.close()

        e_bar_up = [attn_pe_bar]
        e_bar_dn = [w_outproj_last[0]]
        msc = ExitStack()
        with msc:
            upg = [msc.enter_context(nc.sbuf_tensor("upg%d" % i,
                                                    [128, 16 * 128], BF16))
                   for i in range(2)]
            upu = [msc.enter_context(nc.sbuf_tensor("upu%d" % i,
                                                    [128, 16 * 128], BF16))
                   for i in range(2)]
            dnb = [msc.enter_context(nc.sbuf_tensor("dnb%d" % i,
                                                    [128, 11 * 512], BF16))
                   for i in range(4)]
            sgT = msc.enter_context(nc.sbuf_tensor("sgT", [128, T], BF16))
            apT = msc.enter_context(nc.sbuf_tensor("apT", [128, NPAIR * T],
                                                   BF16))
            pse = ExitStack()
            with pse:
                psUg = pse.enter_context(nc.psum_tensor("psUg", [128, T],
                                                        F32))
                psUu = pse.enter_context(nc.psum_tensor("psUu", [128, T],
                                                        F32))
                psDn = [pse.enter_context(nc.psum_tensor("psDn%d" % i,
                                                         [128, 512], F32))
                        for i in range(2)]

                w_upg_use = [None, None]
                w_upu_use = [None, None]
                w_dnb_use = [None, None, None, None]
                w_silu = w_gate_last_ae
                w_ap_prev = w_gate_last_ve
                w_dn_evac = [None, None]
                w_acc_final = [None] * NT

                for e in range(n_experts):
                    up_ap, g_off, u_off, dn_ap, cwcol = etab(e)
                    up3 = up_ap.rearrange("(c p) o -> p c o", p=128)
                    dn3 = dn_ap.rearrange("(ic p) o -> p ic o", p=128)
                    w_apm = [None] * NPAIR
                    for i in range(NPAIR):
                        s = i % 2
                        chg = dch["upg%d" % s]
                        chu = dch["upu%d" % s]
                        P.emit("gpsimd",
                               (lambda s_=s, off=g_off + i * 128, u3=up3:
                                (lambda g: g.dma_start(
                                    out=upg[s_][:].rearrange(
                                        "p (c j) -> p c j", c=16),
                                    in_=u3[:, :, off:off + 128])))(),
                               waits=[w_upg_use[s]] + e_bar_up,
                               bump=chg, amt=16)
                        w_g = (chg, chg.count)
                        P.emit("gpsimd",
                               (lambda s_=s, off=u_off + i * 128, u3=up3:
                                (lambda g: g.dma_start(
                                    out=upu[s_][:].rearrange(
                                        "p (c j) -> p c j", c=16),
                                    in_=u3[:, :, off:off + 128])))(),
                               waits=[w_upu_use[s]] + e_bar_up,
                               bump=chu, amt=16)
                        w_u = (chu, chu.count)
                        glast = None
                        for d in range(16):
                            for q2 in range(2):
                                glast = P.emit(
                                    "tensor",
                                    (lambda s_=s, d_=d, q2_=q2:
                                     (lambda t: t.matmul(
                                         psUg[:, q2_ * 512:q2_ * 512 + 512],
                                         upg[s_][:, d_ * 128:(d_ + 1) * 128],
                                         x1T[:, d_ * T + q2_ * 512:
                                             d_ * T + q2_ * 512 + 512],
                                         start=(d_ == 0), stop=(d_ == 15))))(),
                                    waits=[w_g, w_x1T_ready, w_silu],
                                    bump=(c_pe if (d == 15 and q2 == 1)
                                          else None))
                        w_upg_use[s] = glast
                        ulast = None
                        for d in range(16):
                            for q2 in range(2):
                                ulast = P.emit(
                                    "tensor",
                                    (lambda s_=s, d_=d, q2_=q2:
                                     (lambda t: t.matmul(
                                         psUu[:, q2_ * 512:q2_ * 512 + 512],
                                         upu[s_][:, d_ * 128:(d_ + 1) * 128],
                                         x1T[:, d_ * T + q2_ * 512:
                                             d_ * T + q2_ * 512 + 512],
                                         start=(d_ == 0), stop=(d_ == 15))))(),
                                    waits=[w_u, w_ap_prev, w_x1T_ready],
                                    bump=(c_pe if (d == 15 and q2 == 1)
                                          else None))
                        w_upu_use[s] = ulast
                        w_silu = P.emit(
                            "scalar",
                            lambda sc: sc.activation(sgT[:], psUg[:],
                                                     AF.Silu),
                            waits=[glast, w_ap_prev], bump=c_ae)
                        w_ap_prev = P.emit(
                            "vector",
                            (lambda i_=i: (lambda v: v.tensor_mul(
                                apT[:, i_ * T:(i_ + 1) * T],
                                psUu[:], sgT[:])))(),
                            waits=[ulast, w_silu], bump=c_ve)
                        w_apm[i] = w_ap_prev
                    w_apT_ready = w_apm[NPAIR - 1]
                    last_e = (e == n_experts - 1)
                    w_ds = [None] * 4
                    for oc in range(4):
                        s = oc % 4
                        ch = dch["dnb%d" % s]
                        P.emit("gpsimd",
                               (lambda s_=s, oc_=oc, d3=dn3:
                                (lambda g: g.dma_start(
                                    out=dnb[s_][:].rearrange(
                                        "p (ic j) -> p ic j", ic=NPAIR),
                                    in_=d3[:, :, oc_ * 512:
                                           (oc_ + 1) * 512])))(),
                               waits=[w_dnb_use[s]] + e_bar_dn,
                               bump=ch, amt=16)
                        w_ds[oc] = (ch, ch.count)
                    loop = ([(oc, t_) for oc in range(4) for t_ in range(NT)]
                            if not last_e else
                            [(oc, t_) for t_ in range(NT) for oc in range(4)])
                    for oc, t_ in loop:
                        s = oc % 4
                        b = (t_ + oc) % 2 if last_e else t_ % 2
                        dlast = None
                        for ic in range(NPAIR):
                            dlast = P.emit(
                                "tensor",
                                (lambda s_=s, b_=b, ic_=ic, t__=t_:
                                 (lambda t: t.matmul(
                                     psDn[b_][:],
                                     apT[:, ic_ * T + t__ * 128:
                                         ic_ * T + t__ * 128 + 128],
                                     dnb[s_][:, ic_ * 512:
                                             (ic_ + 1) * 512],
                                     start=(ic_ == 0),
                                     stop=(ic_ == NPAIR - 1))))(),
                                waits=[w_ds[oc], w_apT_ready, w_dn_evac[b]],
                                bump=(c_pe if ic == NPAIR - 1 else None))
                        w_dnb_use[s] = dlast
                        w_dn_evac[b] = P.emit(
                            "vector",
                            (lambda b_=b, t__=t_, oc_=oc, cwc=cwcol,
                                    le=last_e:
                             (lambda v: v.scalar_tensor_tensor(
                                 acc[:, t__ * D + oc_ * 512:
                                     t__ * D + oc_ * 512 + 512],
                                 psDn[b_][:],
                                 cw[:, t__ * NE + cwc:
                                    t__ * NE + cwc + 1],
                                 acc[:, t__ * D + oc_ * 512:
                                     t__ * D + oc_ * 512 + 512],
                                 op0=ALU.mult, op1=ALU.add,
                                 accum_out=(gscA[:, t__ * 64 + 60 + oc_:
                                                 t__ * 64 + 61 + oc_]
                                            if le else None))))(),
                            waits=[dlast, w_cw_ready, w_x1_ready],
                            bump=c_ve)
                        if last_e and oc == 3:
                            w_acc_final[t_] = w_dn_evac[b]
                            wv = w_dn_evac[b]
                            aslc = acc[:, t_ * D:(t_ + 1) * D]
                            tk1 = P.emit("vector",
                                (lambda t__=t_: (lambda v: v.reduce_sum(
                                    SV(t__, 0),
                                    gscA[:, t__ * 64 + 60:t__ * 64 + 64],
                                    axis=AX.X)))(),
                                waits=[wv], bump=c_ve)
                            tks = P.emit("scalar",
                                (lambda t__=t_: (lambda sc: sc.activation(
                                    xb2[t__ % 2][:],
                                    acc[:, t__ * D:(t__ + 1) * D],
                                    AF.Square, accum_out=SV(t__, 1))))(),
                                waits=[wv, w_xb2_rd[t_ % 2],
                                       w_xb2_wr[t_ % 2]], bump=c_ae)
                            w_xb2_wr[t_ % 2] = tks
                            tk1 = P.emit("vector",
                                (lambda t__=t_: (lambda v: v.tensor_scalar(
                                    SV(t__, 2), SV(t__, 0), -1.0 / D, None,
                                    op0=ALU.mult)))(),
                                waits=[tk1], bump=c_ve)
                            tk1 = P.emit("vector",
                                (lambda t__=t_: (lambda v: v.tensor_mul(
                                    SV(t__, 3), SV(t__, 2), SV(t__, 2))))(),
                                waits=[tk1], bump=c_ve)
                            tk1 = P.emit("vector",
                                (lambda t__=t_:
                                 (lambda v: v.scalar_tensor_tensor(
                                    SV(t__, 4), SV(t__, 1), 1.0 / D,
                                    SV(t__, 3), op0=ALU.mult,
                                    op1=ALU.subtract)))(),
                                waits=[tk1, tks], bump=c_ve)
                            tk1 = P.emit("scalar",
                                (lambda t__=t_: (lambda sc: sc.activation(
                                    SV(t__, 5), SV(t__, 4), AF.Ln,
                                    bias=cst[:, 1:2], scale=1.0)))(),
                                waits=[tk1, w_cst], bump=c_ae)
                            tk1 = P.emit("scalar",
                                (lambda t__=t_: (lambda sc: sc.activation(
                                    SV(t__, 5), SV(t__, 5), AF.Exp,
                                    bias=0.0, scale=cst[:, 2:3])))(),
                                waits=[tk1], bump=c_ae)
                            tk1 = P.emit("vector",
                                (lambda t__=t_: (lambda v: v.tensor_scalar(
                                    acc[:, t__ * D:(t__ + 1) * D],
                                    acc[:, t__ * D:(t__ + 1) * D],
                                    SV(t__, 2), SV(t__, 5),
                                    op0=ALU.add, op1=ALU.mult)))(),
                                waits=[tk1], bump=c_ve)
                            P.emit("sync",
                                   (lambda t__=t_: (lambda sy: sy.dma_start(
                                       out=y[t__ * 128:(t__ + 1) * 128, :],
                                       in_=acc[:, t__ * D:
                                               (t__ + 1) * D])))(),
                                   waits=[tk1], bump=c_so, amt=16)

                P.emit("sync", None, waits=[(c_so, c_so.count)])

        for name, ch in P.chans.items():
            if ch.count > 0:
                ch.sem = es.enter_context(nc.semaphore(name))
        blk = es.enter_context(nc.Block())

        def replay(engname):
            def body(eng):
                for ws, fn, bump, amt in P.ops[engname]:
                    for chn, tok in ws:
                        if chn.sem is not None:
                            eng.wait_ge(chn.sem, tok)
                    if fn is None:
                        continue
                    ins = fn(eng)
                    if bump is not None and bump.sem is not None:
                        ins.then_inc(bump.sem, amt)
            return body

        blk.tensor(replay("tensor"))
        blk.vector(replay("vector"))
        blk.scalar(replay("scalar"))
        blk.gpsimd(replay("gpsimd"))
        blk.sync(replay("sync"))
    return nc



def _prep(inputs):
    f32 = lambda a: np.ascontiguousarray(np.asarray(a, dtype=np.float32))
    x = f32(inputs["L0_attn_x"])
    wp = f32(inputs["L0_attn_w_project"])
    wo = f32(inputs["L0_attn_w_out"])
    gw = f32(inputs["L0_ffn_gate_w"])
    ue = f32(inputs["L0_ffn_expert_up_w"])
    de = f32(inputs["L0_ffn_expert_down_w"])
    su = f32(inputs["L0_ffn_shared_up_w"])
    sd = f32(inputs["L0_ffn_shared_down_w"])

    com = {
        "Wq": np.ascontiguousarray(
            wp[0:2048].reshape(NH, 128, D).transpose(0, 2, 1)),
        "Wk": np.ascontiguousarray(
            wp[2048:4096].reshape(NH, 128, D).transpose(0, 2, 1)),
        "Wv4": np.ascontiguousarray(
            wp[4096:6144].reshape(4, 512, D).transpose(0, 2, 1)),
        "Wo": np.ascontiguousarray(wo.T),
        "Gw": np.ascontiguousarray(gw.T),
        "Ue": np.ascontiguousarray(ue.transpose(0, 2, 1)),
        "Dn": np.ascontiguousarray(de.transpose(0, 2, 1)),
        "Su": np.ascontiguousarray(su.T),
        "Sd": np.ascontiguousarray(sd.T),
    }

    masks = []
    for w in range(2):
        M = np.zeros((2, 128, 8 * 512), dtype=np.float32)
        for qc in range(2):
            for ktm in range(8):
                kt = ktm + (8 if qc == 1 else 0)
                gk = kt * 128 + np.arange(128)
                for sq in range(4):
                    slot = qc * 4 + sq
                    gq = PERMS[w][slot] * 128 + np.arange(128)
                    M[qc, :, ktm * 512 + sq * 128:
                      ktm * 512 + sq * 128 + 128] = (
                        gk[:, None] <= gq[None, :]).astype(np.float32)
        masks.append(M)

    qrows = []
    for w in range(2):
        qrows.append(np.concatenate(
            [np.arange(p * 128, (p + 1) * 128) for p in PERMS[w]]))

    in_maps = []
    for c in range(8):
        b, w = c // 2, c % 2
        m = dict(com)
        m["xq"] = np.ascontiguousarray(x[b][qrows[w]])
        m["xk"] = np.ascontiguousarray(x[b])
        m["Mq"] = masks[w]
        in_maps.append(m)
    return in_maps, qrows


def kernel(**inputs):
    in_maps, qrows = _prep(inputs)
    nc = build_program()
    res = run_bass_kernel_spmd(nc, in_maps, core_ids=list(range(8)))
    out = np.empty((4, 2048, 2048), dtype=np.float32)
    for c in range(8):
        b, w = c // 2, c % 2
        out[b][qrows[w]] = res.results[c]["y"]
    return out


# version 08
# speedup vs baseline: 1.1696x; 1.1589x over previous
import numpy as np
from contextlib import ExitStack

import concourse.bass as bass
import concourse.mybir as mybir
from concourse.bass_core_rust import IndirectOffsetOnAxis
from concourse.bass_utils import run_bass_kernel_spmd

F32 = mybir.dt.float32
BF16 = mybir.dt.bfloat16
I32 = mybir.dt.int32
ALU = mybir.AluOpType
AF = mybir.ActivationFunctionType
AX = mybir.AxisListType

D = 2048
NH = 16
T = 1024
NT = 8
KT = 16
NE = 18
NPAIR = 11
CAP = 512
NST = CAP // 128
SCALE_QK = float(1.0 / np.sqrt(128.0))
EXP_BIAS = -10.0

PERMS = [[0, 2, 4, 6, 9, 11, 13, 15], [1, 3, 5, 7, 8, 10, 12, 14]]
QC_KTS = [8, 16]

ENGINES = ("tensor", "vector", "scalar", "gpsimd", "sync")


class Chan:
    __slots__ = ("name", "count", "sem")

    def __init__(self, name):
        self.name = name
        self.count = 0
        self.sem = None


class Prog:
    def __init__(self):
        self.ops = {e: [] for e in ENGINES}
        self.hwm = {e: {} for e in ENGINES}
        self.chans = {}

    def chan(self, name):
        if name not in self.chans:
            self.chans[name] = Chan(name)
        return self.chans[name]

    def emit(self, eng, fn, waits=(), bump=None, amt=1):
        ws = []
        for wt in waits:
            if wt is None:
                continue
            ch, tok = wt
            if ch is None or tok is None or tok <= 0:
                continue
            if self.hwm[eng].get(ch.name, 0) >= tok:
                continue
            self.hwm[eng][ch.name] = tok
            ws.append((ch, tok))
        tup = None
        if bump is not None:
            bump.count += amt
            tup = (bump, bump.count)
        self.ops[eng].append((ws, fn, bump, amt))
        return tup


def build_program(dbg=(), n_experts=NE, d_oc=4, d_t=NT):
    nc = bass.Bass("TRN2", target_bir_lowering=False)
    P = Prog()

    xq = nc.dram_tensor("xq", [T, D], F32, kind="ExternalInput").ap()
    xk = nc.dram_tensor("xk", [2048, D], F32, kind="ExternalInput").ap()
    Mq = nc.dram_tensor("Mq", [2, 128, 8 * 512], F32, kind="ExternalInput").ap()
    Wq = nc.dram_tensor("Wq", [NH, D, 128], F32, kind="ExternalInput").ap()
    Wk = nc.dram_tensor("Wk", [NH, D, 128], F32, kind="ExternalInput").ap()
    Wv4 = nc.dram_tensor("Wv4", [4, D, 512], F32, kind="ExternalInput").ap()
    Wo = nc.dram_tensor("Wo", [D, D], F32, kind="ExternalInput").ap()
    Gw = nc.dram_tensor("Gw", [D, 16], F32, kind="ExternalInput").ap()
    Ue = nc.dram_tensor("Ue", [16, D, 2816], F32, kind="ExternalInput").ap()
    Dn = nc.dram_tensor("Dn", [16, 1408, D], F32, kind="ExternalInput").ap()
    Su = nc.dram_tensor("Su", [D, 5632], F32, kind="ExternalInput").ap()
    Sd = nc.dram_tensor("Sd", [2816, D], F32, kind="ExternalInput").ap()
    y = nc.dram_tensor("y", [T, D], F32, kind="ExternalOutput").ap()
    x1d = nc.dram_tensor("x1d", [T, D], BF16, kind="Internal").ap()
    douts = nc.dram_tensor("douts", [16 * CAP, D], F32, kind="Internal").ap()
    dbg_out = {}
    for name in dbg:
        shp = {"accD": [T, D], "x1": [T, D], "cw": [128, NT * NE],
               "oh": [128, NH * T], "qf": [128, NH * T], "kf": [128, 2048],
               "vt": [128, KT * 512], "acc0": [T, D], "sv": [128, 16],
               "wob_d": [128, 16 * 512], "pd": [128, 512],
               "oh2": [128, NH * T], "ev0": [128, 512]}[name]
        dbg_out[name] = nc.dram_tensor("dbg_" + name, shp, F32,
                                       kind="ExternalOutput").ap()

    def etab(e):
        if e < 16:
            return (Ue[e], 0, 1408, Dn[e], e)
        if e == 16:
            return (Su, 0, 2816, Sd[0:1408, :], 16)
        return (Su, 1408, 4224, Sd[1408:2816, :], 17)

    c_pe = P.chan("pe")
    c_ve = P.chan("ve")
    c_ae = P.chan("ae")
    c_gp = P.chan("gp")
    c_sd = P.chan("sd")
    c_so = P.chan("so")
    dch = {k: P.chan("d_" + k) for k in
           ("xb0", "xb1", "wqb0", "wqb1", "wkb0", "wkb1", "wvb", "mld",
            "wob0", "wob1", "gwb", "upg0", "upg1", "upu0", "upu1",
            "dnb0", "dnb1", "dnb2", "dnb3", "gx0", "gx1", "cmb", "dox",
            "x1w")}

    def bar():
        return [(c_pe, c_pe.count), (c_ve, c_ve.count), (c_ae, c_ae.count)]

    es = ExitStack()
    with es:
        ident = es.enter_context(nc.sbuf_tensor("ident", [128, 128], BF16))
        ones = es.enter_context(nc.sbuf_tensor("ones", [128, 128], BF16))
        gwb = es.enter_context(nc.sbuf_tensor("gwb", [128, 256], BF16))
        cst = es.enter_context(nc.sbuf_tensor("cst", [128, 8], F32))
        zz4 = es.enter_context(nc.sbuf_tensor("zz4", [128, 4], F32))
        triL = es.enter_context(nc.sbuf_tensor("triL", [128, 128], BF16))
        T8c = es.enter_context(nc.sbuf_tensor("T8c", [128, 64], BF16))
        T8b = es.enter_context(nc.sbuf_tensor("T8b", [8, 1024], BF16))
        iotaF = es.enter_context(nc.sbuf_tensor("iotaF", [128, CAP], F32))
        e512r = es.enter_context(nc.sbuf_tensor("e512r", [128, 16], F32))
        rhsT = es.enter_context(nc.sbuf_tensor("rhsT", [128, NT * 18], BF16))

        P.emit("gpsimd", lambda g: g.memset(ident[:], 0.0), bump=c_gp)
        w_ident = P.emit(
            "gpsimd",
            lambda g: g.affine_select(
                out=ident[:], in_=ident[:], compare_op=ALU.not_equal,
                fill=1.0, base=0, pattern=[[-1, 128]], channel_multiplier=1),
            waits=[(c_gp, 1)], bump=c_gp)
        w_ones = P.emit("gpsimd", lambda g: g.memset(ones[:], 1.0), bump=c_gp)
        w_zz4 = P.emit("gpsimd", lambda g: g.memset(zz4[:], 0.0), bump=c_gp)
        P.emit("gpsimd", lambda g: g.memset(triL[:], 0.0), bump=c_gp)
        w_triL = P.emit("gpsimd", lambda g: g.affine_select(
            out=triL[:], in_=triL[:], compare_op=ALU.is_ge, fill=1.0,
            base=0, pattern=[[-1, 128]], channel_multiplier=1), bump=c_gp)
        P.emit("gpsimd", lambda g: g.memset(T8c[:], 0.0), bump=c_gp)
        w_T8c = P.emit("gpsimd", lambda g: g.affine_select(
            out=T8c[:], in_=T8c[:], compare_op=ALU.not_equal, fill=1.0,
            base=0, pattern=[[-1, 8], [1, 8]], channel_multiplier=0),
            bump=c_gp)
        P.emit("gpsimd", lambda g: g.memset(T8b[:], 0.0), bump=c_gp)
        w_T8b = P.emit("gpsimd", lambda g: g.affine_select(
            out=T8b[:], in_=T8b[:], compare_op=ALU.not_equal, fill=1.0,
            base=0, pattern=[[-1, 8], [0, 128]], channel_multiplier=1),
            bump=c_gp)
        w_iotaF = P.emit("gpsimd", lambda g: g.iota(
            iotaF[:], pattern=[[1, CAP]], base=0, channel_multiplier=0,
            allow_small_or_imprecise_dtypes=True),
            bump=c_gp)
        w_e512 = P.emit("gpsimd", lambda g: g.iota(
            e512r[:], pattern=[[512, 16]], base=0, channel_multiplier=0,
            allow_small_or_imprecise_dtypes=True),
            bump=c_gp)
        w_rt1 = P.emit("gpsimd", lambda g: g.iota(
            rhsT[:].rearrange("p (t c) -> p t c", c=18)[:, :, 16:17],
            pattern=[[0, NT], [0, 1]], base=0, channel_multiplier=1,
            allow_small_or_imprecise_dtypes=True),
            bump=c_gp)
        w_rt2 = P.emit("gpsimd", lambda g: g.iota(
            rhsT[:].rearrange("p (t c) -> p t c", c=18)[:, :, 17:18],
            pattern=[[1, NT], [0, 1]], base=0, channel_multiplier=0,
            allow_small_or_imprecise_dtypes=True),
            bump=c_gp)
        for i, val in enumerate((EXP_BIAS, 1e-5, -0.5, SCALE_QK)):
            w_cst = P.emit("gpsimd",
                           (lambda i_=i, v_=val:
                            (lambda g: g.memset(cst[:, i_:i_ + 1], v_)))(),
                           bump=c_gp)

        qstackO = ExitStack()
        OHf = qstackO.enter_context(nc.sbuf_tensor("OHf", [128, NH * T], BF16))
        qstackQ = ExitStack()
        Qf = qstackQ.enter_context(nc.sbuf_tensor("Qf", [128, NH * T], BF16))

        att = ExitStack()
        with att:
            xT = att.enter_context(nc.sbuf_tensor("xT", [128, 16 * 2048], BF16))
            Mb = att.enter_context(nc.sbuf_tensor("Mb", [128, 2 * 4096], BF16))

            w_xT_ready = None
            w_xqT_ready = None
            w_Qf_ready = None
            w_Q_lastmm = None
            psc = ExitStack()
            with psc:
                psT = [psc.enter_context(nc.psum_tensor("psT%d" % i,
                                                        [128, 1024], BF16))
                       for i in range(2)]
                xbq = ExitStack()
                with xbq:
                    xb = [xbq.enter_context(nc.sbuf_tensor(
                        "xb%d" % i, [128, D], BF16)) for i in range(2)]
                    w_xb_use = [None, None]
                    w_psT_evac = [None, None]

                    def build_T(src_ap, dst, kt_idx, ncols, parts):
                        s = kt_idx % 2
                        ch = dch["xb%d" % s]
                        P.emit("gpsimd",
                               (lambda a=src_ap, s_=s:
                                (lambda g: g.dma_start(out=xb[s_][:], in_=a)))(),
                               waits=[w_xb_use[s]], bump=ch, amt=16)
                        w_dma = (ch, ch.count)
                        dst3 = dst.rearrange("p (dc k) -> p dc k", k=ncols)
                        for half in range(2):
                            b = half
                            last = None
                            for j in range(8):
                                dc = half * 8 + j
                                last = P.emit(
                                    "tensor",
                                    (lambda b_=b, j_=j, s_=s, dc_=dc:
                                     (lambda t: t.transpose(
                                         psT[b_][:, j_ * 128:(j_ + 1) * 128],
                                         xb[s_][:, dc_ * 128:(dc_ + 1) * 128],
                                         ident[:])))(),
                                    waits=[w_dma, w_ident, w_psT_evac[b]],
                                    bump=(c_pe if j == 7 else None))
                            w_xb_use[s] = last
                            ov = dst3[:, half * 8:half * 8 + 8,
                                      kt_idx * 128:kt_idx * 128 + 128]
                            iv = psT[b][:, 0:1024].rearrange(
                                "p (dc k) -> p dc k", k=128)
                            w_psT_evac[b] = P.emit(
                                "vector",
                                (lambda o=ov, i=iv:
                                 (lambda v: v.tensor_copy(o, i)))(),
                                waits=[last], bump=c_ve)
                            parts.append(w_psT_evac[b])

                    pts = []
                    for kt in range(KT):
                        build_T(xk[kt * 128:(kt + 1) * 128, :], xT[:], kt,
                                2048, pts)
                    w_xT_ready = pts[-1]
                    for qc in range(2):
                        P.emit("gpsimd",
                               (lambda qc_=qc: (lambda g: g.dma_start(
                                   out=Mb[:, qc_ * 4096:(qc_ + 1) * 4096],
                                   in_=Mq[qc_])))(),
                               bump=dch["mld"], amt=16)
                    w_mask_ld = (dch["mld"], dch["mld"].count)
                    pts2 = []
                    xqT = xbq.enter_context(
                        nc.sbuf_tensor("xqT", [128, 16 * T], BF16))
                    for qt in range(NT):
                        build_T(xq[qt * 128:(qt + 1) * 128, :], xqT[:], qt,
                                T, pts2)
                    w_xqT_ready = pts2[-1]

                    wqb = [xbq.enter_context(nc.sbuf_tensor(
                        "wqb%d" % i, [128, 16 * 128], BF16)) for i in range(2)]
                    psq = ExitStack()
                    with psq:
                        psQ = [psq.enter_context(nc.psum_tensor(
                            "psQ%d" % i, [128, 512], F32)) for i in range(4)]
                        w_wqb_use = [None, None]
                        w_psQ_evac = [None, None, None, None]
                        for h in range(NH):
                            s = h % 2
                            ch = dch["wqb%d" % s]
                            P.emit("gpsimd",
                                   (lambda s_=s, h_=h: (lambda g: g.dma_start(
                                       out=wqb[s_][:].rearrange(
                                           "p (c j) -> p c j", c=16),
                                       in_=Wq[h_].rearrange(
                                           "(c p) j -> p c j", p=128))))(),
                                   waits=[w_wqb_use[s]], bump=ch, amt=16)
                            w_dma = (ch, ch.count)
                            lastmm = None
                            hb = (h % 2) * 2
                            for d in range(16):
                                for q2 in range(2):
                                    lastmm = P.emit(
                                        "tensor",
                                        (lambda s_=s, d_=d, q2_=q2, hb_=hb:
                                         (lambda t: t.matmul(
                                             psQ[hb_ + q2_][:],
                                             wqb[s_][:, d_ * 128:
                                                     (d_ + 1) * 128],
                                             xqT[:, d_ * T + q2_ * 512:
                                                 d_ * T + q2_ * 512 + 512],
                                             start=(d_ == 0),
                                             stop=(d_ == 15))))(),
                                        waits=[w_dma, w_xqT_ready,
                                               w_psQ_evac[hb + q2]],
                                        bump=(c_pe if d == 15 else None))
                            w_wqb_use[s] = lastmm
                            w_Q_lastmm = lastmm
                            for q2 in range(2):
                                w_psQ_evac[hb + q2] = P.emit(
                                    "scalar",
                                    (lambda h_=h, q2_=q2, hb_=hb:
                                     (lambda sc: sc.mul(
                                        Qf[:, h_ * T + q2_ * 512:
                                           h_ * T + q2_ * 512 + 512],
                                        psQ[hb_ + q2_][:], cst[:, 3:4])))(),
                                    waits=[lastmm, w_cst], bump=c_ae)
                        w_Qf_ready = w_psQ_evac[hb + 1]

            Kf2 = [att.enter_context(nc.sbuf_tensor("Kf%d" % i,
                                                     [128, 2048], BF16))
                   for i in range(2)]
            Vtok = att.enter_context(nc.sbuf_tensor("Vtok", [128, KT * 512],
                                                    BF16))
            PT = att.enter_context(nc.sbuf_tensor("PT", [128, 8 * 512], BF16))
            rZb = att.enter_context(nc.sbuf_tensor("rZb", [128, 512], F32))
            wkb = [att.enter_context(nc.sbuf_tensor("wkb%d" % i,
                                                    [128, 16 * 128], BF16))
                   for i in range(2)]
            wvb = att.enter_context(nc.sbuf_tensor("wvb", [128, 16 * 512],
                                                   BF16))

            bc_bar = [w_Q_lastmm, w_Qf_ready, w_xqT_ready]

            psa = ExitStack()
            with psa:
                psP = [psa.enter_context(nc.psum_tensor("psP%d" % i,
                                                        [128, 512], F32))
                       for i in range(2)]
                psS = psa.enter_context(nc.psum_tensor("psS", [128, 2048],
                                                       F32))
                psZ = psa.enter_context(nc.psum_tensor("psZ", [128, 512], F32))
                psO = psa.enter_context(nc.psum_tensor("psO", [128, 512], F32))

                w_wkb_use = [w_Q_lastmm, w_Q_lastmm]
                w_wvb_use = w_Q_lastmm
                w_mask_last = [None]
                w_xT_last = [None]
                w_psP_evac = [w_Qf_ready, w_Qf_ready]
                w_Vtok_ready = None
                w_recip_prev = None
                w_oh_prev = None
                w_oh_all = []
                w_Kf_ready = [None, None]
                w_ST_last = [None]
                wkb_dma = [None] * (NH + 2)

                def emit_wkb_load(h_):
                    s_ = h_ % 2
                    chk = dch["wkb%d" % s_]
                    P.emit("gpsimd",
                           (lambda s__=s_, h__=h_: (lambda g: g.dma_start(
                               out=wkb[s__][:].rearrange("p (c j) -> p c j",
                                                         c=16),
                               in_=Wk[h__].rearrange("(c p) j -> p c j",
                                                     p=128))))(),
                           waits=[w_wkb_use[s_]] + bc_bar, bump=chk, amt=16)
                    wkb_dma[h_] = (chk, chk.count)

                def kproj_chunk(h_, kc):
                    s_ = h_ % 2
                    b = kc % 2
                    lastmm = None
                    for d in range(16):
                        lastmm = P.emit(
                            "tensor",
                            (lambda s__=s_, b_=b, d_=d, kc_=kc:
                             (lambda t: t.matmul(
                                 psP[b_][:],
                                 wkb[s__][:, d_ * 128:(d_ + 1) * 128],
                                 xT[:, d_ * 2048 + kc_ * 512:
                                    d_ * 2048 + kc_ * 512 + 512],
                                 start=(d_ == 0), stop=(d_ == 15))))(),
                            waits=[wkb_dma[h_], w_psP_evac[b], w_xT_ready],
                            bump=(c_pe if d == 15 else None))
                    w_psP_evac[b] = P.emit(
                        "scalar",
                        (lambda s__=s_, kc_=kc, b_=b: (lambda sc: sc.copy(
                            Kf2[s__][:, kc_ * 512:(kc_ + 1) * 512],
                            psP[b_][:])))(),
                        waits=[lastmm], bump=c_ae)
                    w_wkb_use[s_] = lastmm
                    w_xT_last[0] = lastmm
                    if kc == 3:
                        w_Kf_ready[s_] = w_psP_evac[b]

                emit_wkb_load(0)
                emit_wkb_load(1)
                for kc in range(4):
                    kproj_chunk(0, kc)
                kproj_chunk(1, 0)
                kproj_chunk(1, 1)

                def emit_wvb_load(grp_):
                    P.emit("gpsimd",
                           (lambda g_=grp_: (lambda g: g.dma_start(
                               out=wvb[:].rearrange("p (c j) -> p c j",
                                                    c=16),
                               in_=Wv4[g_].rearrange(
                                   "(c p) j -> p c j", p=128))))(),
                           waits=[w_wvb_use] + bc_bar,
                           bump=dch["wvb"], amt=16)
                    wvb_dma[0] = (dch["wvb"], dch["wvb"].count)

                wvb_dma = [None]
                emit_wvb_load(0)
                for h in range(NH):
                    if h + 2 < NH:
                        emit_wkb_load(h + 2)
                    if h % 4 == 2 and (h + 2) // 4 < 4:
                        emit_wvb_load((h + 2) // 4)
                    if h % 4 == 0:
                        w_dma = wvb_dma[0]
                        vts = []
                        lastmm = None
                        for kt in range(KT):
                            b = kt % 2
                            for d in range(16):
                                lastmm = P.emit(
                                    "tensor",
                                    (lambda b_=b, d_=d, kt_=kt:
                                     (lambda t: t.matmul(
                                         psP[b_][:],
                                         xT[:, d_ * 2048 + kt_ * 128:
                                            d_ * 2048 + kt_ * 128 + 128],
                                         wvb[:, d_ * 512:(d_ + 1) * 512],
                                         start=(d_ == 0), stop=(d_ == 15))))(),
                                    waits=[w_dma, w_psP_evac[b], w_xT_ready],
                                    bump=(c_pe if d == 15 else None))
                            w_psP_evac[b] = P.emit(
                                "vector",
                                (lambda kt_=kt, b_=b: (lambda v: v.tensor_copy(
                                    Vtok[:, kt_ * 512:(kt_ + 1) * 512],
                                    psP[b_][:])))(),
                                waits=[lastmm], bump=c_ve)
                            vts.append(w_psP_evac[b])
                        w_wvb_use = lastmm
                        w_xT_last[0] = lastmm
                        w_Vtok_ready = vts[-1]

                    if h + 1 >= NH:
                        pend = []
                    elif h == 0:
                        pend = [2, 3]
                    else:
                        pend = list(range(4))

                    def inject(n):
                        for _ in range(n):
                            if pend:
                                kproj_chunk(h + 1, pend.pop(0))

                    for qc in range(2):
                        kts = QC_KTS[qc]
                        qoff = qc * 512
                        w_Z_stop = None
                        w_O_stop = None
                        for ktg in range(0, kts, 8):
                            exp_toks = []
                            for kt2 in range(ktg, ktg + 8, 2):
                                half = (kt2 // 2) % 2
                                st_last = None
                                for kt in (kt2, kt2 + 1):
                                    col = half * 1024 + (kt % 2) * 512
                                    st_last = P.emit(
                                        "tensor",
                                        (lambda kt_=kt, h_=h, qoff_=qoff,
                                                col_=col:
                                         (lambda t: t.matmul(
                                             psS[:, col_:col_ + 512],
                                             Kf2[h_ % 2][:, kt_ * 128:
                                                         (kt_ + 1) * 128],
                                             Qf[:, h_ * T + qoff_:
                                                h_ * T + qoff_ + 512],
                                             start=True, stop=True)))(),
                                        waits=[w_Kf_ready[h % 2], w_Qf_ready,
                                               exp_toks[-2] if
                                               len(exp_toks) >= 2 else None,
                                               w_recip_prev, w_oh_prev],
                                        bump=(c_pe if kt == kt2 + 1
                                              else None))
                                w_ST_last[0] = st_last
                                tok_e = P.emit(
                                    "scalar",
                                    (lambda kt2_=kt2, ktg_=ktg, half_=half:
                                     (lambda sc: sc.activation(
                                         PT[:, (kt2_ - ktg_) * 512:
                                            (kt2_ - ktg_) * 512 + 1024],
                                         psS[:, half_ * 1024:
                                             half_ * 1024 + 1024],
                                         AF.Exp, bias=cst[:, 0:1],
                                         scale=1.0)))(),
                                    waits=[st_last, w_cst], bump=c_ae)
                                exp_toks.append(tok_e)
                            inject(2 if (qc == 0) else 1)
                            masked = (qc == 0 and ktg == 0) or \
                                     (qc == 1 and ktg == 8)
                            if masked:
                                mq = 0 if qc == 0 else 1
                                w_mask = P.emit(
                                    "vector",
                                    (lambda mq_=mq: (lambda v: v.tensor_mul(
                                        PT[:, 0:4096], PT[:, 0:4096],
                                        Mb[:, mq_ * 4096:
                                           (mq_ + 1) * 4096])))(),
                                    waits=[exp_toks[-1], w_mask_ld],
                                    bump=c_ve)
                                w_mask_last[0] = w_mask
                                pt_wait = [w_mask]
                            else:
                                pt_wait = [exp_toks[-1]]
                            for kt in range(ktg, ktg + 8):
                                w_Z_stop = P.emit(
                                    "tensor",
                                    (lambda kt_=kt, ktg_=ktg, kts_=kts:
                                     (lambda t: t.matmul(
                                         psZ[:], ones[:],
                                         PT[:, (kt_ - ktg_) * 512:
                                            (kt_ - ktg_) * 512 + 512],
                                         start=(kt_ == 0),
                                         stop=(kt_ == kts_ - 1))))(),
                                    waits=pt_wait + [w_ones, w_recip_prev],
                                    bump=(c_pe if kt == kts - 1 else None))
                            for kt in range(ktg, ktg + 8):
                                w_O_stop = P.emit(
                                    "tensor",
                                    (lambda kt_=kt, ktg_=ktg, kts_=kts, h_=h:
                                     (lambda t: t.matmul(
                                         psO[:],
                                         Vtok[:, kt_ * 512 + (h_ % 4) * 128:
                                              kt_ * 512 + (h_ % 4) * 128
                                              + 128],
                                         PT[:, (kt_ - ktg_) * 512:
                                            (kt_ - ktg_) * 512 + 512],
                                         start=(kt_ == 0),
                                         stop=(kt_ == kts_ - 1))))(),
                                    waits=pt_wait + [w_Vtok_ready, w_oh_prev],
                                    bump=(c_pe if kt == kts - 1 else None))
                        w_recip_prev = P.emit(
                            "vector",
                            lambda v: v.reciprocal(rZb[:], psZ[:]),
                            waits=[w_Z_stop], bump=c_ve)
                        w_oh_prev = P.emit(
                            "vector",
                            (lambda h_=h, qoff_=qoff: (lambda v: v.tensor_mul(
                                OHf[:, h_ * T + qoff_:h_ * T + qoff_ + 512],
                                psO[:], rZb[:])))(),
                            waits=[w_O_stop, w_recip_prev], bump=c_ve)
                        w_oh_all.append(w_oh_prev)
                    inject(4)
                attn_pe_bar = (c_pe, c_pe.count)
                d_psP_guard = [w_psP_evac[0], w_psP_evac[1]]
                w_OHf_ready = w_oh_all[-1]

            if "qf" in dbg_out:
                P.emit("gpsimd", lambda g: g.dma_start(
                    out=dbg_out["qf"], in_=Qf[:]),
                    waits=[w_Qf_ready], bump=c_so, amt=16)
            if "kf" in dbg_out:
                P.emit("gpsimd", lambda g: g.dma_start(
                    out=dbg_out["kf"], in_=Kf[:]),
                    waits=[(c_ae, c_ae.count)], bump=c_so, amt=16)
            if "vt" in dbg_out:
                P.emit("gpsimd", lambda g: g.dma_start(
                    out=dbg_out["vt"], in_=Vtok[:]),
                    waits=[(c_ve, c_ve.count)], bump=c_so, amt=16)
            if "oh" in dbg_out:
                P.emit("gpsimd", lambda g: g.dma_start(
                    out=dbg_out["oh"], in_=OHf[:]),
                    waits=[w_OHf_ready], bump=c_so, amt=16)
        qstackQ.close()

        acc = es.enter_context(nc.sbuf_tensor("acc", [128, NT * D], F32,
                                              side="right"))
        x1T = es.enter_context(nc.sbuf_tensor("x1T", [128, 16 * T], BF16,
                                              side="right"))
        xb2 = [es.enter_context(nc.sbuf_tensor("xb2_%d" % i, [128, D], BF16,
                                               side="right"))
               for i in range(2)]
        cw = es.enter_context(nc.sbuf_tensor("cw", [128, NT * NE], F32,
                                             side="right"))
        sva = es.enter_context(nc.sbuf_tensor("sva", [128, NT * 8], F32,
                                              side="right"))
        gscA = es.enter_context(nc.sbuf_tensor("gscA", [128, NT * 64], F32,
                                               side="right"))
        gscB = es.enter_context(nc.sbuf_tensor("gscB", [128, NT * 64], F32,
                                               side="right"))
        idxS = es.enter_context(nc.sbuf_tensor("idxS", [128, 64], I32,
                                               side="right"))
        wgtS = es.enter_context(nc.sbuf_tensor("wgtS", [128, 64], F32,
                                               side="right"))
        invS = es.enter_context(nc.sbuf_tensor("invS", [128, 32], I32,
                                               side="right"))

        def A_(t_, a, b=None):
            if b is None:
                b = a + 1
            return gscA[:, t_ * 64 + a:t_ * 64 + b]

        def B_(t_, a, b=None):
            if b is None:
                b = a + 1
            return gscB[:, t_ * 64 + a:t_ * 64 + b]

        def SV(t_, j):
            return sva[:, t_ * 8 + j:t_ * 8 + j + 1]

        w_gwb = P.emit(
            "gpsimd",
            lambda g: g.dma_start(
                out=gwb[:].rearrange("p (c j) -> p c j", c=16),
                in_=Gw.rearrange("(c p) j -> p c j", p=128)),
            bump=dch["gwb"], amt=16)

        d_bar = bar()
        w_ae_bar = (c_ae, c_ae.count)
        for t_ in range(NT):
            P.emit("sync",
                   (lambda t__=t_: (lambda sy: sy.dma_start(
                       out=acc[:, t__ * D:(t__ + 1) * D],
                       in_=xq[t__ * 128:(t__ + 1) * 128, :])))(),
                   waits=d_bar, bump=c_sd, amt=16)
        w_acc_ld = (c_sd, c_sd.count)

        w_acc_done = [None] * NT
        w_xb2_rd = [None, None]
        w_xb2_wr = [None, None]
        w_xb2_dma = [None, None]
        w_psTD_evac = [None, None]

        def ln_stats(w_ready, w_pre, pbase):
            s1t, s2t, t1, t2, t3 = [], [], [], [], []
            for t_ in range(NT):
                s1t.append(P.emit(
                    "vector",
                    (lambda t__=t_: (lambda v: v.reduce_sum(
                        SV(t__, 0),
                        gscA[:, t__ * 64 + pbase:t__ * 64 + pbase + 4],
                        axis=AX.X)))(),
                    waits=[w_ready[t_], w_pre], bump=c_ve))
            for t_ in range(NT):
                s2t.append(P.emit(
                    "scalar",
                    (lambda t__=t_: (lambda sc: sc.activation(
                        xb2[t__ % 2][:], acc[:, t__ * D:(t__ + 1) * D],
                        AF.Square, accum_out=SV(t__, 1))))(),
                    waits=[w_ready[t_], w_pre, w_xb2_rd[t_ % 2],
                           w_xb2_wr[t_ % 2]],
                    bump=c_ae))
                w_xb2_wr[t_ % 2] = s2t[-1]
            for t_ in range(NT):
                t1.append(P.emit(
                    "vector",
                    (lambda t__=t_: (lambda v: v.tensor_scalar(
                        SV(t__, 2), SV(t__, 0), -1.0 / D, None,
                        op0=ALU.mult)))(),
                    waits=[s1t[t_]], bump=c_ve))
            for t_ in range(NT):
                t1[t_] = P.emit(
                    "vector",
                    (lambda t__=t_: (lambda v: v.tensor_mul(
                        SV(t__, 3), SV(t__, 2), SV(t__, 2))))(),
                    waits=[t1[t_]], bump=c_ve)
            for t_ in range(NT):
                t2.append(P.emit(
                    "vector",
                    (lambda t__=t_: (lambda v: v.scalar_tensor_tensor(
                        SV(t__, 4), SV(t__, 1), 1.0 / D, SV(t__, 3),
                        op0=ALU.mult, op1=ALU.subtract)))(),
                    waits=[t1[t_], s2t[t_]], bump=c_ve))
            for t_ in range(NT):
                t2[t_] = P.emit(
                    "scalar",
                    (lambda t__=t_: (lambda sc: sc.activation(
                        SV(t__, 5), SV(t__, 4), AF.Ln,
                        bias=cst[:, 1:2], scale=1.0)))(),
                    waits=[t2[t_], w_cst], bump=c_ae)
            for t_ in range(NT):
                t2[t_] = P.emit(
                    "scalar",
                    (lambda t__=t_: (lambda sc: sc.activation(
                        SV(t__, 5), SV(t__, 5), AF.Exp,
                        bias=0.0, scale=cst[:, 2:3])))(),
                    waits=[t2[t_]], bump=c_ae)
            for t_ in range(NT):
                t3.append(P.emit(
                    "vector",
                    (lambda t__=t_: (lambda v: v.tensor_scalar(
                        acc[:, t__ * D:(t__ + 1) * D],
                        acc[:, t__ * D:(t__ + 1) * D],
                        SV(t__, 2), SV(t__, 5),
                        op0=ALU.add, op1=ALU.mult)))(),
                    waits=[t2[t_]], bump=c_ve))
            return t3

        dsc = ExitStack()
        with dsc:
            wob = [dsc.enter_context(nc.sbuf_tensor("wob%d" % i,
                                                    [128, 16 * 512], BF16))
                   for i in range(2)]
            selb = dsc.enter_context(nc.sbuf_tensor("selb", [128, NT * 16],
                                                    BF16))
            pfx = dsc.enter_context(nc.sbuf_tensor("pfx", [128, NT * 16],
                                                   F32))
            sposM = dsc.enter_context(nc.sbuf_tensor("sposM", [128, NT * 16],
                                                     F32))
            Gbuf = dsc.enter_context(nc.sbuf_tensor("Gbuf", [128, NT * CAP],
                                                    BF16))
            lbuf = dsc.enter_context(nc.sbuf_tensor("lbuf", [128, 40], F32))
            cnt8 = dsc.enter_context(nc.sbuf_tensor("cnt8", [8, 32], F32))
            cnt8b = dsc.enter_context(nc.sbuf_tensor("cnt8b", [8, 32], BF16))
            psd = ExitStack()
            with psd:
                psTD = [psd.enter_context(nc.psum_tensor("psTD%d" % i,
                                                         [128, 1024], BF16))
                        for i in range(2)]
                psG = [psd.enter_context(nc.psum_tensor("psG%d" % i,
                                                        [128, 16], F32))
                       for i in range(2)]
                psdA = ExitStack()
                psD2 = [psdA.enter_context(nc.psum_tensor("psD2%d" % i,
                                                          [128, 512], F32))
                        for i in range(2)]

                w_wob_use = [[w_ST_last[0]], [w_ST_last[0]]]
                w_outproj_last = [None]
                w_psD2_evac = [None, None]
                first_mm = True
                for oc in range(d_oc):
                    s = oc % 2
                    ch = dch["wob%d" % s]
                    P.emit("gpsimd",
                           (lambda s_=s, oc_=oc: (lambda g: g.dma_start(
                               out=wob[s_][:].rearrange("p (c j) -> p c j",
                                                        c=16),
                               in_=Wo.rearrange("(c p) o -> p c o",
                                                p=128)[:, :, oc_ * 512:
                                                       (oc_ + 1) * 512])))(),
                           waits=list(w_wob_use[s]), bump=ch, amt=16)
                    w_dma = (ch, ch.count)
                    for t_ in range(NT):
                        b = t_ % 2
                        lastmm = None
                        for hc in range(16):
                            lastmm = P.emit(
                                "tensor",
                                (lambda s_=s, b_=b, hc_=hc, t__=t_:
                                 (lambda t: t.matmul(
                                     psD2[b_][:],
                                     OHf[:, hc_ * T + t__ * 128:
                                         hc_ * T + t__ * 128 + 128],
                                     wob[s_][:, hc_ * 512:(hc_ + 1) * 512],
                                     start=(hc_ == 0), stop=(hc_ == 15))))(),
                                waits=[w_dma, w_OHf_ready, w_psD2_evac[b]]
                                + (d_psP_guard if first_mm else []),
                                bump=(c_pe if hc == 15 else None))
                            first_mm = False
                        w_wob_use[s] = [lastmm]
                        w_outproj_last[0] = lastmm
                        w_psD2_evac[b] = P.emit(
                            "vector",
                            (lambda t__=t_, oc_=oc, b_=b:
                             (lambda v: v.scalar_tensor_tensor(
                                 acc[:, t__ * D + oc_ * 512:
                                     t__ * D + oc_ * 512 + 512],
                                 psD2[b_][:], 1.0,
                                 acc[:, t__ * D + oc_ * 512:
                                     t__ * D + oc_ * 512 + 512],
                                 op0=ALU.mult, op1=ALU.add,
                                 accum_out=gscA[:, t__ * 64 + 56 + oc_:
                                                t__ * 64 + 57 + oc_])))(),
                            waits=[lastmm, w_acc_ld], bump=c_ve)
                        if oc == d_oc - 1:
                            w_acc_done[t_] = w_psD2_evac[b]

                w_dump = None
                if "accD" in dbg_out:
                    for t_ in range(NT):
                        w_dump = P.emit(
                            "sync",
                            (lambda t__=t_: (lambda sy: sy.dma_start(
                                out=dbg_out["accD"][t__ * 128:
                                                    (t__ + 1) * 128, :],
                                in_=acc[:, t__ * D:(t__ + 1) * D])))(),
                            waits=[w_acc_done[t_]], bump=c_so, amt=16)

                psdA.close()
                psX = psd.enter_context(nc.psum_tensor("psX", [128, 512],
                                                       F32))
                psL = [psd.enter_context(nc.psum_tensor("psL%d" % i,
                                                        [128, 32], F32))
                       for i in range(2)]

                w_x1_tiles = ln_stats(w_acc_done, w_dump, 56)
                w_x1_ready = w_x1_tiles[-1]

                x1T3 = x1T[:].rearrange("p (dc k) -> p dc k", k=T)
                w_psG_rd = [[None, None], [None, None]]
                w_x1T_evac = [None] * NT
                w_cw_tiles = [None] * NT
                w_selb_t = [None] * NT
                w_cwb_t = [None] * NT
                w_pfx_evac = [None]
                w_cnt_mm = [None]
                cnt_ps = psL[0]
                w_gsA_init = [None] * NT
                for t_ in range(NT):
                    w_gsA_init[t_] = P.emit(
                        "vector",
                        (lambda t__=t_: (lambda v: v.memset(
                            A_(t__, 20, 24), -1e9)))(),
                        waits=d_bar, bump=c_ve)

                def gate_stage(pair, fn_t, eng="vector", waits_t=None,
                               bump=c_ve):
                    out = {}
                    for t_ in pair:
                        out[t_] = P.emit(eng, fn_t(t_),
                                         waits=(waits_t(t_) if waits_t
                                                else []),
                                         bump=bump)
                    return out

                for p0 in range(0, NT, 2):
                    pair = (p0, p0 + 1)
                    for t_ in pair:
                        w_cast = P.emit(
                            "scalar",
                            (lambda t__=t_: (lambda sc: sc.copy(
                                xb2[t__ % 2][:],
                                acc[:, t__ * D:(t__ + 1) * D])))(),
                            waits=[w_x1_tiles[t_], w_xb2_rd[t_ % 2],
                                   w_xb2_wr[t_ % 2], w_xb2_dma[t_ % 2]],
                            bump=c_ae)
                        w_xb2_wr[t_ % 2] = w_cast
                        P.emit("sync",
                               (lambda t__=t_: (lambda sy: sy.dma_start(
                                   out=x1d[t__ * 128:(t__ + 1) * 128, :],
                                   in_=xb2[t__ % 2][:])))(),
                               waits=[w_cast], bump=dch["x1w"], amt=16)
                        w_xb2_dma[t_ % 2] = (dch["x1w"], dch["x1w"].count)
                        for half in range(2):
                            b = half
                            last = None
                            for j in range(8):
                                dc = half * 8 + j
                                last = P.emit(
                                    "tensor",
                                    (lambda b_=b, j_=j, dc_=dc, t__=t_:
                                     (lambda t: t.transpose(
                                         psTD[b_][:, j_ * 128:(j_ + 1) * 128],
                                         xb2[t__ % 2][:, dc_ * 128:
                                                      (dc_ + 1) * 128],
                                         ident[:])))(),
                                    waits=[w_cast, w_psTD_evac[b], w_ident],
                                    bump=(c_pe if j == 7 else None))
                            w_xb2_rd[t_ % 2] = last
                            ov = x1T3[:, half * 8:half * 8 + 8,
                                      t_ * 128:t_ * 128 + 128]
                            iv = psTD[b][:, 0:1024].rearrange(
                                "p (dc k) -> p dc k", k=128)
                            w_psTD_evac[b] = P.emit(
                                "vector",
                                (lambda o=ov, i=iv:
                                 (lambda v: v.tensor_copy(o, i)))(),
                                waits=[last], bump=c_ve)
                        w_x1T_evac[t_] = w_psTD_evac[1]
                    w_gmm = {}
                    for t_ in pair:
                        lastg = None
                        for d in range(16):
                            lastg = P.emit(
                                "tensor",
                                (lambda d_=d, t__=t_: (lambda t: t.matmul(
                                    psG[t__ % 2][:],
                                    x1T[:, d_ * T + t__ * 128:
                                        d_ * T + t__ * 128 + 128],
                                    gwb[:, d_ * 16:(d_ + 1) * 16],
                                    start=(d_ == 0), stop=(d_ == 15))))(),
                                waits=[w_gwb, w_x1T_evac[t_],
                                       w_psG_rd[t_ % 2][0],
                                       w_psG_rd[t_ % 2][1]],
                                bump=(c_pe if d == 15 else None))
                        w_gmm[t_] = lastg
                    tk = gate_stage(pair, lambda t_: (
                        lambda t__=t_: (lambda v: v.reduce_max(
                            A_(t__, 52), psG[t__ % 2][:], axis=AX.X,
                            negate=True)))(),
                        waits_t=lambda t_: [w_gmm[t_]])
                    for t_ in pair:
                        w_psG_rd[t_ % 2][0] = tk[t_]
                    tke = {}
                    for t_ in pair:
                        tke[t_] = P.emit(
                            "scalar",
                            (lambda t__=t_: (lambda sc: sc.activation(
                                A_(t__, 0, 16), psG[t__ % 2][:], AF.Exp,
                                bias=A_(t__, 52), scale=1.0,
                                accum_out=A_(t__, 53))))(),
                            waits=[tk[t_]], bump=c_ae)
                        w_psG_rd[t_ % 2][1] = tke[t_]
                    tk = gate_stage(pair, lambda t_: (
                        lambda t__=t_: (lambda v: v.reciprocal(
                            A_(t__, 54), A_(t__, 53))))(),
                        waits_t=lambda t_: [tke[t_]])
                    tk = gate_stage(pair, lambda t_: (
                        lambda t__=t_: (lambda v: v.tensor_single_scalar(
                            A_(t__, 0, 16), A_(t__, 0, 16), A_(t__, 54),
                            ALU.mult)))(),
                        waits_t=lambda t_: [tk[t_]])
                    for j in range(4):
                        tk = gate_stage(pair, lambda t_: (
                            lambda t__=t_, j_=j: (lambda v: v.reduce_max(
                                A_(t__, 16 + j_), A_(t__, 4 * j_, 4 * j_ + 4),
                                axis=AX.X)))(),
                            waits_t=lambda t_: [tk[t_]])
                    tk = gate_stage(pair, lambda t_: (
                        lambda t__=t_: (lambda v: v.max(
                            A_(t__, 24, 32), A_(t__, 16, 24))))(),
                        waits_t=lambda t_: [tk[t_], w_gsA_init[t_]])
                    tk = gate_stage(pair, lambda t_: (
                        lambda t__=t_: (lambda v: v.tensor_single_scalar(
                            A_(t__, 32, 36), A_(t__, 16, 20), A_(t__, 26),
                            ALU.is_gt)))(),
                        waits_t=lambda t_: [tk[t_]])
                    for j in range(4):
                        tk = gate_stage(pair, lambda t_: (
                            lambda t__=t_, j_=j:
                            (lambda v: v.tensor_single_scalar(
                                A_(t__, 36 + 4 * j_, 40 + 4 * j_), zz4[:],
                                A_(t__, 32 + j_), ALU.add)))(),
                            waits_t=lambda t_: [tk[t_], w_zz4])
                    tk = gate_stage(pair, lambda t_: (
                        lambda t__=t_: (lambda v: v.tensor_mul(
                            B_(t__, 0, 16), A_(t__, 0, 16),
                            A_(t__, 36, 52))))(),
                        waits_t=lambda t_: [tk[t_]])
                    tk = gate_stage(pair, lambda t_: (
                        lambda t__=t_: (lambda v: v.max(
                            B_(t__, 16, 24), B_(t__, 0, 16))))(),
                        waits_t=lambda t_: [tk[t_]])
                    tk = gate_stage(pair, lambda t_: (
                        lambda t__=t_: (lambda v: v.tensor_single_scalar(
                            B_(t__, 24, 40), B_(t__, 0, 16), B_(t__, 20),
                            ALU.is_gt)))(),
                        waits_t=lambda t_: [tk[t_]])
                    tk = gate_stage(pair, lambda t_: (
                        lambda t__=t_: (lambda v: v.tensor_mul(
                            B_(t__, 40, 56), B_(t__, 0, 16),
                            B_(t__, 24, 40))))(),
                        waits_t=lambda t_: [tk[t_]])
                    tk = gate_stage(pair, lambda t_: (
                        lambda t__=t_: (lambda v: v.reduce_sum(
                            B_(t__, 56), B_(t__, 40, 56), axis=AX.X)))(),
                        waits_t=lambda t_: [tk[t_]])
                    tk = gate_stage(pair, lambda t_: (
                        lambda t__=t_: (lambda v: v.tensor_single_scalar(
                            B_(t__, 56), B_(t__, 56), 1e-20, ALU.add)))(),
                        waits_t=lambda t_: [tk[t_]])
                    tk = gate_stage(pair, lambda t_: (
                        lambda t__=t_: (lambda v: v.reciprocal(
                            B_(t__, 56), B_(t__, 56))))(),
                        waits_t=lambda t_: [tk[t_]])
                    tk = gate_stage(pair, lambda t_: (
                        lambda t__=t_: (lambda v: v.tensor_single_scalar(
                            cw[:, t__ * NE:t__ * NE + 16], B_(t__, 40, 56),
                            B_(t__, 56), ALU.mult)))(),
                        waits_t=lambda t_: [tk[t_]])
                    for t_ in pair:
                        w_cw_tiles[t_] = tk[t_]
                    for t_ in pair:
                        w_selb = P.emit(
                            "vector",
                            (lambda t__=t_: (lambda v: v.tensor_copy(
                                selb[:, t__ * 16:(t__ + 1) * 16],
                                gscB[:, t__ * 64 + 24:t__ * 64 + 40])))(),
                            waits=[w_cw_tiles[t_]], bump=c_ve)
                        w_cwb = P.emit(
                            "vector",
                            (lambda t__=t_: (lambda v: v.tensor_copy(
                                rhsT[:, t__ * 18:t__ * 18 + 16],
                                cw[:, t__ * NE:t__ * NE + 16])))(),
                            waits=[w_cw_tiles[t_], w_rt2], bump=c_ve)
                        w_selb_t[t_] = w_selb
                        w_cwb_t[t_] = w_cwb
                        lastpf = P.emit(
                            "tensor",
                            (lambda t__=t_: (lambda t: t.matmul(
                                psX[:, 0:16], triL[:],
                                selb[:, t__ * 16:(t__ + 1) * 16],
                                start=True, stop=True)))(),
                            waits=[w_triL, w_selb, w_pfx_evac[0]],
                            bump=c_pe)
                        w_pfx_evac[0] = P.emit(
                            "vector",
                            (lambda t__=t_: (lambda v: v.tensor_copy(
                                pfx[:, t__ * 16:(t__ + 1) * 16],
                                psX[:, 0:16])))(),
                            waits=[lastpf], bump=c_ve)
                        w_cnt_mm[0] = P.emit(
                            "tensor",
                            (lambda t__=t_: (lambda t: t.matmul(
                                cnt_ps[0:8, 0:16],
                                T8c[:, t__ * 8:(t__ + 1) * 8],
                                selb[:, t__ * 16:(t__ + 1) * 16],
                                start=(t__ == 0), stop=(t__ == 7))))(),
                            waits=[w_T8c, w_selb, w_pfx_evac[0]],
                            bump=(c_pe if t_ == 7 else None))

                w_cw_ready = P.emit(
                    "vector",
                    lambda v: v.memset(
                        cw[:].rearrange("p (t e) -> p t e",
                                        e=NE)[:, :, 16:18], 1.0),
                    waits=[w_cw_tiles[-1]], bump=c_ve)
                w_x1T_ready = w_psTD_evac[1]
                w_gate_last_ve = w_cw_ready
                w_gate_last_ae = tke[pair[-1]]

                w_cnt_ev = P.emit(
                    "vector",
                    lambda v: v.tensor_copy(cnt8[0:8, 0:16],
                                            cnt_ps[0:8, 0:16]),
                    waits=[w_cnt_mm[0]], bump=c_ve)
                w_cnt_b = P.emit(
                    "vector",
                    lambda v: v.tensor_copy(cnt8b[0:8, 0:16], cnt8[0:8, 0:16]),
                    waits=[w_cnt_ev], bump=c_ve)
                w_base_mm = P.emit(
                    "tensor",
                    lambda t: t.matmul(psX[0:8, 0:16], triL[0:8, 0:8],
                                       cnt8b[0:8, 0:16], start=True,
                                       stop=True),
                    waits=[w_cnt_b, w_pfx_evac[0]], bump=c_pe)
                w_base_ev = P.emit(
                    "vector",
                    lambda v: v.tensor_copy(cnt8[0:8, 16:32], psX[0:8, 0:16]),
                    waits=[w_base_mm], bump=c_ve)
                w_base_b = P.emit(
                    "vector",
                    lambda v: v.tensor_copy(cnt8b[0:8, 16:32],
                                            cnt8[0:8, 16:32]),
                    waits=[w_base_ev], bump=c_ve)
                w_spos_t = [None] * NT
                w_inv_done = [None] * NT
                w_psX_rd = [w_base_ev]
                for t_ in range(NT):
                    w_bc = P.emit(
                        "tensor",
                        (lambda t__=t_: (lambda t: t.matmul(
                            psX[:, 0:16], T8b[0:8, t__ * 128:(t__ + 1) * 128],
                            cnt8b[0:8, 16:32], start=True, stop=True)))(),
                        waits=[w_T8b, w_base_b, w_psX_rd[0]], bump=c_pe)
                    w_sp = P.emit(
                        "vector",
                        (lambda t__=t_: (lambda v: v.tensor_add(
                            pfx[:, t__ * 16:(t__ + 1) * 16],
                            pfx[:, t__ * 16:(t__ + 1) * 16],
                            psX[:, 0:16])))(),
                        waits=[w_bc], bump=c_ve)
                    w_psX_rd[0] = w_sp
                    w_iv = P.emit(
                        "vector",
                        (lambda t__=t_: (lambda v: v.tensor_add(
                            gscB[:, t__ * 64 + 40:t__ * 64 + 56],
                            pfx[:, t__ * 16:(t__ + 1) * 16], e512r[:])))(),
                        waits=[w_sp, w_e512], bump=c_ve)
                    m1 = P.emit(
                        "vector",
                        (lambda t__=t_: (lambda v: v.tensor_single_scalar(
                            gscA[:, t__ * 64:t__ * 64 + 16],
                            pfx[:, t__ * 16:(t__ + 1) * 16], 65536.0,
                            ALU.add)))(),
                        waits=[w_sp], bump=c_ve)
                    m2 = P.emit(
                        "vector",
                        (lambda t__=t_: (lambda v: v.tensor_mul(
                            gscA[:, t__ * 64:t__ * 64 + 16],
                            gscA[:, t__ * 64:t__ * 64 + 16],
                            gscB[:, t__ * 64 + 24:t__ * 64 + 40])))(),
                        waits=[m1], bump=c_ve)
                    m3 = P.emit(
                        "vector",
                        (lambda t__=t_: (lambda v: v.tensor_single_scalar(
                            sposM[:, t__ * 16:(t__ + 1) * 16],
                            gscA[:, t__ * 64:t__ * 64 + 16], 65536.0,
                            ALU.subtract)))(),
                        waits=[m2], bump=c_ve)
                    w_spos_t[t_] = m3
                    wk = w_iv
                    for k in range(4):
                        wk = P.emit(
                            "vector",
                            (lambda t__=t_, k_=k:
                             (lambda v: v.tensor_single_scalar(
                                gscA[:, t__ * 64 + 16:t__ * 64 + 32],
                                gscB[:, t__ * 64:t__ * 64 + 16],
                                gscB[:, t__ * 64 + 17 + k_:
                                     t__ * 64 + 18 + k_], ALU.is_gt)))(),
                            waits=[wk], bump=c_ve)
                        wk = P.emit(
                            "vector",
                            (lambda t__=t_, k_=k:
                             (lambda v: v.tensor_single_scalar(
                                gscA[:, t__ * 64 + 36:t__ * 64 + 52],
                                gscB[:, t__ * 64:t__ * 64 + 16],
                                gscB[:, t__ * 64 + 16 + k_:
                                     t__ * 64 + 17 + k_], ALU.is_gt)))(),
                            waits=[wk], bump=c_ve)
                        wk = P.emit(
                            "vector",
                            (lambda t__=t_: (lambda v: v.tensor_sub(
                                gscA[:, t__ * 64 + 16:t__ * 64 + 32],
                                gscA[:, t__ * 64 + 16:t__ * 64 + 32],
                                gscA[:, t__ * 64 + 36:t__ * 64 + 52])))(),
                            waits=[wk], bump=c_ve)
                        wk = P.emit(
                            "vector",
                            (lambda t__=t_: (lambda v: v.tensor_mul(
                                gscA[:, t__ * 64 + 36:t__ * 64 + 52],
                                gscA[:, t__ * 64 + 16:t__ * 64 + 32],
                                gscB[:, t__ * 64 + 40:t__ * 64 + 56])))(),
                            waits=[wk], bump=c_ve)
                        wk = P.emit(
                            "vector",
                            (lambda t__=t_, k_=k: (lambda v: v.reduce_sum(
                                lbuf[:, 32 + k_:33 + k_],
                                gscA[:, t__ * 64 + 36:t__ * 64 + 52],
                                axis=AX.X)))(),
                            waits=[wk], bump=c_ve)
                        wk = P.emit(
                            "vector",
                            (lambda t__=t_, k_=k: (lambda v: v.tensor_copy(
                                invS[:, t__ * 4 + k_:t__ * 4 + k_ + 1],
                                lbuf[:, 32 + k_:33 + k_])))(),
                            waits=[wk], bump=c_ve)
                    w_inv_done[t_] = wk
                w_Gt = [None] * NT
                w_G_user = [None]
                w_psL_ev = [w_cnt_ev, None]
                w_idx_ready = [None] * 16
                for e in range(16):
                    for t_ in range(NT):
                        w_Gt[t_] = P.emit(
                            "vector",
                            (lambda t__=t_, e_=e:
                             (lambda v: v.tensor_single_scalar(
                                Gbuf[:, t__ * CAP:t__ * CAP + CAP],
                                iotaF[:, 0:CAP],
                                sposM[:, t__ * 16 + e_:t__ * 16 + e_ + 1],
                                ALU.is_equal)))(),
                            waits=[w_spos_t[t_], w_iotaF, w_G_user[0]],
                            bump=c_ve)
                    for sc in range(NST):
                        b = sc % 2
                        lastmm = None
                        for t_ in range(NT):
                            lastmm = P.emit(
                                "tensor",
                                (lambda t__=t_, sc_=sc, b_=b:
                                 (lambda t: t.matmul(
                                    psL[b_][:, 0:18],
                                    Gbuf[:, t__ * CAP + sc_ * 128:
                                         t__ * CAP + sc_ * 128 + 128],
                                    rhsT[:, t__ * 18:(t__ + 1) * 18],
                                    start=(t__ == 0), stop=(t__ == 7))))(),
                                waits=[w_Gt[t_], w_cwb_t[t_], w_rt1, w_rt2,
                                       w_psL_ev[b]],
                                bump=(c_pe if t_ == 7 else None))
                        w_G_user[0] = lastmm
                        ev1 = P.emit(
                            "vector",
                            (lambda b_=b: (lambda v: v.tensor_copy(
                                lbuf[:, 0:18], psL[b_][:, 0:18])))(),
                            waits=[lastmm], bump=c_ve)
                        w_psL_ev[b] = ev1
                        w_id = P.emit(
                            "vector",
                            lambda v: v.scalar_tensor_tensor(
                                lbuf[:, 18:19], lbuf[:, 17:18], 128.0,
                                lbuf[:, 16:17], op0=ALU.mult, op1=ALU.add),
                            waits=[ev1], bump=c_ve)
                        w_id = P.emit(
                            "vector",
                            (lambda e_=e, sc_=sc: (lambda v: v.tensor_copy(
                                idxS[:, e_ * 4 + sc_:e_ * 4 + sc_ + 1],
                                lbuf[:, 18:19])))(),
                            waits=[w_id], bump=c_ve)
                        w_id = P.emit(
                            "vector",
                            (lambda e_=e, sc_=sc: (lambda v: v.tensor_copy(
                                wgtS[:, e_ * 4 + sc_:e_ * 4 + sc_ + 1],
                                lbuf[:, e_:e_ + 1])))(),
                            waits=[w_id], bump=c_ve)
                    w_idx_ready[e] = w_id

            if "x1" in dbg_out:
                for t_ in range(NT):
                    P.emit("sync",
                           (lambda t__=t_: (lambda sy: sy.dma_start(
                               out=dbg_out["x1"][t__ * 128:(t__ + 1) * 128, :],
                               in_=acc[:, t__ * D:(t__ + 1) * D])))(),
                           waits=[w_x1_ready], bump=c_so, amt=16)
            if "cw" in dbg_out:
                P.emit("sync", lambda sy: sy.dma_start(
                    out=dbg_out["cw"], in_=cw[:]),
                    waits=[w_cw_ready], bump=c_so, amt=16)
        qstackO.close()

        w_x1d_all = (dch["x1w"], dch["x1w"].count)

        e_bar_up = [w_outproj_last[0]]
        msc = ExitStack()
        with msc:
            upg = [msc.enter_context(nc.sbuf_tensor("upg%d" % i,
                                                    [128, 16 * 128], BF16))
                   for i in range(2)]
            upu = [msc.enter_context(nc.sbuf_tensor("upu%d" % i,
                                                    [128, 16 * 128], BF16))
                   for i in range(2)]
            dnb = [msc.enter_context(nc.sbuf_tensor("dnb%d" % i,
                                                    [128, 11 * 512], BF16))
                   for i in range(2)]
            sgT = msc.enter_context(nc.sbuf_tensor("sgT", [128, 512], BF16))
            apT = msc.enter_context(nc.sbuf_tensor("apT", [128, NPAIR * 512],
                                                   BF16))
            xg = [msc.enter_context(nc.sbuf_tensor("xg%d" % i, [128, D],
                                                   BF16)) for i in range(2)]
            xgT = msc.enter_context(nc.sbuf_tensor("xgT", [128, 16 * 512],
                                                   BF16))
            dob = [msc.enter_context(nc.sbuf_tensor("dob%d" % i, [128, 512],
                                                    F32)) for i in range(2)]
            pse = ExitStack()
            with pse:
                psUg = pse.enter_context(nc.psum_tensor("psUg", [128, 512],
                                                        F32))
                psUu = pse.enter_context(nc.psum_tensor("psUu", [128, 512],
                                                        F32))
                psDn = [pse.enter_context(nc.psum_tensor("psDn%d" % i,
                                                         [128, 512], F32))
                        for i in range(2)]
                psTE = [pse.enter_context(nc.psum_tensor("psTE%d" % i,
                                                         [128, 1024], BF16))
                        for i in range(2)]

                w_upg_use = [None, None]
                w_upu_use = [None, None]
                w_dnb_use = [None, None]
                w_silu = w_gate_last_ae
                w_ap_prev = w_gate_last_ve
                w_dn_evac = [None, None]
                w_xg_user = [None, None]
                w_psTE_ev = [None, None]
                w_dob_dma = [None, None]
                w_xgT3 = xgT[:].rearrange("p (dc k) -> p dc k", k=512)

                def up_down(e, up_ap, g_off, u_off, dn_ap, rhs_base,
                            sparse, th=0, cwcol=16, last_sh=False):
                    nonlocal w_silu, w_ap_prev
                    up3 = up_ap.rearrange("(c p) o -> p c o", p=128)
                    dn3 = dn_ap.rearrange("(ic p) o -> p ic o", p=128)
                    if sparse:
                        for st in range(NST):
                            sx = st % 2
                            chx = dch["gx%d" % sx]
                            P.emit("gpsimd",
                                   (lambda sx_=sx, e_=e, st_=st:
                                    (lambda g: g.indirect_dma_start(
                                        out=xg[sx_][:], out_offset=None,
                                        in_=x1d,
                                        in_offset=IndirectOffsetOnAxis(
                                            idxS[:, e_ * 4 + st_:
                                                 e_ * 4 + st_ + 1], 0))))(),
                                   waits=[w_idx_ready[e], w_x1d_all,
                                          w_xg_user[sx]] + e_bar_up,
                                   bump=chx, amt=16)
                            w_gx = (chx, chx.count)
                            for half in range(2):
                                bt = half
                                last = None
                                for j in range(8):
                                    dc = half * 8 + j
                                    last = P.emit(
                                        "tensor",
                                        (lambda bt_=bt, j_=j, dc_=dc, sx_=sx:
                                         (lambda t: t.transpose(
                                            psTE[bt_][:, j_ * 128:
                                                      (j_ + 1) * 128],
                                            xg[sx_][:, dc_ * 128:
                                                    (dc_ + 1) * 128],
                                            ident[:])))(),
                                        waits=[w_gx, w_psTE_ev[bt],
                                               w_ident],
                                        bump=(c_pe if j == 7 else None))
                                w_xg_user[sx] = last
                                ov = w_xgT3[:, half * 8:half * 8 + 8,
                                            st * 128:st * 128 + 128]
                                iv = psTE[bt][:, 0:1024].rearrange(
                                    "p (dc k) -> p dc k", k=128)
                                w_psTE_ev[bt] = P.emit(
                                    "vector",
                                    (lambda o=ov, i=iv:
                                     (lambda v: v.tensor_copy(o, i)))(),
                                    waits=[last], bump=c_ve)
                        w_act_rdy = w_psTE_ev[1]
                    else:
                        w_act_rdy = w_x1T_ready
                    for i in range(NPAIR):
                        s = i % 2
                        chg = dch["upg%d" % s]
                        chu = dch["upu%d" % s]
                        P.emit("gpsimd",
                               (lambda s_=s, off=g_off + i * 128, u3=up3:
                                (lambda g: g.dma_start(
                                    out=upg[s_][:].rearrange(
                                        "p (c j) -> p c j", c=16),
                                    in_=u3[:, :, off:off + 128])))(),
                               waits=[w_upg_use[s]] + e_bar_up,
                               bump=chg, amt=16)
                        w_g = (chg, chg.count)
                        P.emit("gpsimd",
                               (lambda s_=s, off=u_off + i * 128, u3=up3:
                                (lambda g: g.dma_start(
                                    out=upu[s_][:].rearrange(
                                        "p (c j) -> p c j", c=16),
                                    in_=u3[:, :, off:off + 128])))(),
                               waits=[w_upu_use[s]] + e_bar_up,
                               bump=chu, amt=16)
                        w_u = (chu, chu.count)
                        glast = None
                        ulast = None
                        for d in range(16):
                            rhs = (w_xgT3 and None) or None
                            glast = P.emit(
                                "tensor",
                                (lambda s_=s, d_=d, sp=sparse, th_=th:
                                 (lambda t: t.matmul(
                                    psUg[:],
                                    upg[s_][:, d_ * 128:(d_ + 1) * 128],
                                    (xgT[:, d_ * 512:(d_ + 1) * 512]
                                     if sp else
                                     x1T[:, d_ * T + th_ * 512:
                                         d_ * T + th_ * 512 + 512]),
                                    start=(d_ == 0), stop=(d_ == 15))))(),
                                waits=[w_g, w_act_rdy, w_silu],
                                bump=(c_pe if d == 15 else None))
                        for d in range(16):
                            ulast = P.emit(
                                "tensor",
                                (lambda s_=s, d_=d, sp=sparse, th_=th:
                                 (lambda t: t.matmul(
                                    psUu[:],
                                    upu[s_][:, d_ * 128:(d_ + 1) * 128],
                                    (xgT[:, d_ * 512:(d_ + 1) * 512]
                                     if sp else
                                     x1T[:, d_ * T + th_ * 512:
                                         d_ * T + th_ * 512 + 512]),
                                    start=(d_ == 0), stop=(d_ == 15))))(),
                                waits=[w_u, w_ap_prev, w_act_rdy],
                                bump=(c_pe if d == 15 else None))
                        w_upg_use[s] = glast
                        w_upu_use[s] = ulast
                        w_silu = P.emit(
                            "scalar",
                            lambda sc: sc.activation(sgT[:], psUg[:],
                                                     AF.Silu),
                            waits=[glast, w_ap_prev], bump=c_ae)
                        w_ap_prev = P.emit(
                            "vector",
                            (lambda i_=i: (lambda v: v.tensor_mul(
                                apT[:, i_ * 512:(i_ + 1) * 512],
                                psUu[:], sgT[:])))(),
                            waits=[ulast, w_silu], bump=c_ve)
                    w_apT_ready = w_ap_prev
                    for oc in range(4):
                        s = oc % 2
                        ch = dch["dnb%d" % s]
                        P.emit("gpsimd",
                               (lambda s_=s, oc_=oc, d3=dn3:
                                (lambda g: g.dma_start(
                                    out=dnb[s_][:].rearrange(
                                        "p (ic j) -> p ic j", ic=NPAIR),
                                    in_=d3[:, :, oc_ * 512:
                                           (oc_ + 1) * 512])))(),
                               waits=[w_dnb_use[s]] + e_bar_up,
                               bump=ch, amt=16)
                        w_d = (ch, ch.count)
                        for st in range(NST):
                            b = st % 2
                            dlast = None
                            for ic in range(NPAIR):
                                dlast = P.emit(
                                    "tensor",
                                    (lambda s_=s, b_=b, ic_=ic, st_=st:
                                     (lambda t: t.matmul(
                                        psDn[b_][:],
                                        apT[:, ic_ * 512 + st_ * 128:
                                            ic_ * 512 + st_ * 128 + 128],
                                        dnb[s_][:, ic_ * 512:
                                                (ic_ + 1) * 512],
                                        start=(ic_ == 0),
                                        stop=(ic_ == NPAIR - 1))))(),
                                    waits=[w_d, w_apT_ready, w_dn_evac[b]],
                                    bump=(c_pe if ic == NPAIR - 1 else None))
                            w_dnb_use[s] = dlast
                            if sparse:
                                w_ev = P.emit(
                                    "vector",
                                    (lambda b_=b, e_=e, st_=st:
                                     (lambda v: v.tensor_single_scalar(
                                        dob[b_][:], psDn[b_][:],
                                        wgtS[:, e_ * 4 + st_:
                                             e_ * 4 + st_ + 1],
                                        ALU.mult)))(),
                                    waits=[dlast, w_idx_ready[e],
                                           w_dob_dma[b]], bump=c_ve)
                                w_dn_evac[b] = w_ev
                                P.emit("sync",
                                       (lambda b_=b, e_=e, st_=st, oc_=oc:
                                        (lambda sy: sy.dma_start(
                                            out=douts[e_ * CAP + st_ * 128:
                                                      e_ * CAP + st_ * 128
                                                      + 128,
                                                      oc_ * 512:
                                                      (oc_ + 1) * 512],
                                            in_=dob[b_][:])))(),
                                       waits=[w_ev], bump=dch["dox"],
                                       amt=16)
                                w_dob_dma[b] = (dch["dox"],
                                                dch["dox"].count)
                            else:
                                t_ = th * 4 + st
                                w_dn_evac[b] = P.emit(
                                    "vector",
                                    (lambda b_=b, t__=t_, oc_=oc, cwc=cwcol:
                                     (lambda v: v.scalar_tensor_tensor(
                                        acc[:, t__ * D + oc_ * 512:
                                            t__ * D + oc_ * 512 + 512],
                                        psDn[b_][:],
                                        cw[:, t__ * NE + cwc:
                                           t__ * NE + cwc + 1],
                                        acc[:, t__ * D + oc_ * 512:
                                            t__ * D + oc_ * 512 + 512],
                                        op0=ALU.mult, op1=ALU.add,
                                        accum_out=(
                                            gscA[:, t__ * 64 + 60 + oc_:
                                                 t__ * 64 + 61 + oc_]
                                            if last_sh else None))))(),
                                    waits=[dlast, w_cw_ready, w_x1_ready,
                                           w_combine[t_]], bump=c_ve)
                                if last_sh and oc == 3:
                                    emit_LN2_store(t_, w_dn_evac[b])

                def emit_LN2_store(t_, wv):
                    tk1 = P.emit("vector",
                        (lambda t__=t_: (lambda v: v.reduce_sum(
                            SV(t__, 0),
                            gscA[:, t__ * 64 + 60:t__ * 64 + 64],
                            axis=AX.X)))(),
                        waits=[wv], bump=c_ve)
                    tks = P.emit("scalar",
                        (lambda t__=t_: (lambda sc: sc.activation(
                            xb2[t__ % 2][:],
                            acc[:, t__ * D:(t__ + 1) * D],
                            AF.Square, accum_out=SV(t__, 1))))(),
                        waits=[wv, w_xb2_rd[t_ % 2], w_xb2_wr[t_ % 2],
                               w_xb2_dma[t_ % 2]], bump=c_ae)
                    w_xb2_wr[t_ % 2] = tks
                    tk1 = P.emit("vector",
                        (lambda t__=t_: (lambda v: v.tensor_scalar(
                            SV(t__, 2), SV(t__, 0), -1.0 / D, None,
                            op0=ALU.mult)))(),
                        waits=[tk1], bump=c_ve)
                    tk1 = P.emit("vector",
                        (lambda t__=t_: (lambda v: v.tensor_mul(
                            SV(t__, 3), SV(t__, 2), SV(t__, 2))))(),
                        waits=[tk1], bump=c_ve)
                    tk1 = P.emit("vector",
                        (lambda t__=t_: (lambda v: v.scalar_tensor_tensor(
                            SV(t__, 4), SV(t__, 1), 1.0 / D, SV(t__, 3),
                            op0=ALU.mult, op1=ALU.subtract)))(),
                        waits=[tk1, tks], bump=c_ve)
                    tk1 = P.emit("scalar",
                        (lambda t__=t_: (lambda sc: sc.activation(
                            SV(t__, 5), SV(t__, 4), AF.Ln,
                            bias=cst[:, 1:2], scale=1.0)))(),
                        waits=[tk1, w_cst], bump=c_ae)
                    tk1 = P.emit("scalar",
                        (lambda t__=t_: (lambda sc: sc.activation(
                            SV(t__, 5), SV(t__, 5), AF.Exp,
                            bias=0.0, scale=cst[:, 2:3])))(),
                        waits=[tk1], bump=c_ae)
                    tk1 = P.emit("vector",
                        (lambda t__=t_: (lambda v: v.tensor_scalar(
                            acc[:, t__ * D:(t__ + 1) * D],
                            acc[:, t__ * D:(t__ + 1) * D],
                            SV(t__, 2), SV(t__, 5),
                            op0=ALU.add, op1=ALU.mult)))(),
                        waits=[tk1], bump=c_ve)
                    P.emit("sync",
                           (lambda t__=t_: (lambda sy: sy.dma_start(
                               out=y[t__ * 128:(t__ + 1) * 128, :],
                               in_=acc[:, t__ * D:(t__ + 1) * D])))(),
                           waits=[tk1], bump=c_so, amt=16)

                n_sparse = min(16, n_experts)
                for e in range(n_sparse):
                    up_down(e, Ue[e], 0, 1408, Dn[e], None, True)
                w_dox_all = (dch["dox"], dch["dox"].count)

                w_combine = [None] * NT
                if n_sparse > 0:
                    for t_ in range(NT):
                        wc = None
                        for k in range(4):
                            wc = P.emit(
                                "gpsimd",
                                (lambda t__=t_, k_=k:
                                 (lambda g: g.indirect_dma_start(
                                    out=acc[:, t__ * D:(t__ + 1) * D],
                                    out_offset=None,
                                    in_=douts,
                                    in_offset=IndirectOffsetOnAxis(
                                        invS[:, t__ * 4 + k_:
                                             t__ * 4 + k_ + 1], 0),
                                    compute_op=ALU.add)))(),
                                waits=[w_dox_all, w_inv_done[t_],
                                       w_x1_ready, w_acc_done[t_]],
                                bump=dch["cmb"], amt=16)
                        w_combine[t_] = wc

                if n_experts >= 17:
                    up_down(16, Su, 0, 2816, Sd[0:1408, :], None, False,
                            th=0, cwcol=16)
                    up_down(16, Su, 0, 2816, Sd[0:1408, :], None, False,
                            th=1, cwcol=16)
                if n_experts >= 18:
                    up_down(17, Su, 1408, 4224, Sd[1408:2816, :], None,
                            False, th=0, cwcol=17, last_sh=True)
                    up_down(17, Su, 1408, 4224, Sd[1408:2816, :], None,
                            False, th=1, cwcol=17, last_sh=True)
                P.emit("sync", None, waits=[(c_so, c_so.count)])

        for name, ch in P.chans.items():
            if ch.count > 0:
                ch.sem = es.enter_context(nc.semaphore(name))
        blk = es.enter_context(nc.Block())

        def replay(engname):
            def body(eng):
                for ws, fn, bump, amt in P.ops[engname]:
                    for chn, tok in ws:
                        if chn.sem is not None:
                            eng.wait_ge(chn.sem, tok)
                    if fn is None:
                        continue
                    ins = fn(eng)
                    if bump is not None and bump.sem is not None:
                        ins.then_inc(bump.sem, amt)
            return body

        blk.tensor(replay("tensor"))
        blk.vector(replay("vector"))
        blk.scalar(replay("scalar"))
        blk.gpsimd(replay("gpsimd"))
        blk.sync(replay("sync"))
    return nc



def _prep(inputs):
    f32 = lambda a: np.ascontiguousarray(np.asarray(a, dtype=np.float32))
    x = f32(inputs["L0_attn_x"])
    wp = f32(inputs["L0_attn_w_project"])
    wo = f32(inputs["L0_attn_w_out"])
    gw = f32(inputs["L0_ffn_gate_w"])
    ue = f32(inputs["L0_ffn_expert_up_w"])
    de = f32(inputs["L0_ffn_expert_down_w"])
    su = f32(inputs["L0_ffn_shared_up_w"])
    sd = f32(inputs["L0_ffn_shared_down_w"])

    com = {
        "Wq": np.ascontiguousarray(
            wp[0:2048].reshape(NH, 128, D).transpose(0, 2, 1)),
        "Wk": np.ascontiguousarray(
            wp[2048:4096].reshape(NH, 128, D).transpose(0, 2, 1)),
        "Wv4": np.ascontiguousarray(
            wp[4096:6144].reshape(4, 512, D).transpose(0, 2, 1)),
        "Wo": np.ascontiguousarray(wo.T),
        "Gw": np.ascontiguousarray(gw.T),
        "Ue": np.ascontiguousarray(ue.transpose(0, 2, 1)),
        "Dn": np.ascontiguousarray(de.transpose(0, 2, 1)),
        "Su": np.ascontiguousarray(su.T),
        "Sd": np.ascontiguousarray(sd.T),
    }

    masks = []
    for w in range(2):
        M = np.zeros((2, 128, 8 * 512), dtype=np.float32)
        for qc in range(2):
            for ktm in range(8):
                kt = ktm + (8 if qc == 1 else 0)
                gk = kt * 128 + np.arange(128)
                for sq in range(4):
                    slot = qc * 4 + sq
                    gq = PERMS[w][slot] * 128 + np.arange(128)
                    M[qc, :, ktm * 512 + sq * 128:
                      ktm * 512 + sq * 128 + 128] = (
                        gk[:, None] <= gq[None, :]).astype(np.float32)
        masks.append(M)

    qrows = []
    for w in range(2):
        qrows.append(np.concatenate(
            [np.arange(p * 128, (p + 1) * 128) for p in PERMS[w]]))

    in_maps = []
    for c in range(8):
        b, w = c // 2, c % 2
        m = dict(com)
        m["xq"] = np.ascontiguousarray(x[b][qrows[w]])
        m["xk"] = np.ascontiguousarray(x[b])
        m["Mq"] = masks[w]
        in_maps.append(m)
    return in_maps, qrows


def kernel(**inputs):
    in_maps, qrows = _prep(inputs)
    nc = build_program()
    res = run_bass_kernel_spmd(nc, in_maps, core_ids=list(range(8)))
    out = np.empty((4, 2048, 2048), dtype=np.float32)
    for c in range(8):
        b, w = c // 2, c % 2
        out[b][qrows[w]] = res.results[c]["y"]
    return out
